# version 04
# baseline (speedup 1.0000x reference)
import numpy as np
import ml_dtypes
import concourse.bass as bass
import concourse.mybir as mybir
from concourse.bass_utils import run_bass_kernel_spmd

F32 = mybir.dt.float32
BF16 = mybir.dt.bfloat16
AF = mybir.ActivationFunctionType
ALU = mybir.AluOpType
AX = mybir.AxisListType

NCORES = 8
B_PER = 2
T = 2048
D = 1024
H = 16
HD = 64
NT = T // 128
NC = D // 128
NQC = 4
QC = 512
YPAD = 2

ENGINES = ("tensor", "vector", "scalar", "sync", "gpsimd")


class Plan:

    def __init__(self):
        self.steps = []
        self.events = {}
        self.counts = {e: 0 for e in ENGINES}

    def op(self, eng, fn, wait=(), inc=None):
        if inc is not None:
            amt = 16 if eng in ("sync", "gpsimd") else 1
            self.counts[eng] += amt
            assert inc not in self.events, f"duplicate event {inc}"
            self.events[inc] = (eng, self.counts[eng])
        self.steps.append((eng, fn, tuple(w for w in wait if w), inc))

    def emit(self, eng_name, eng_obj, sems):
        last = {}
        passed = 0
        for eng, fn, waits, inc in self.steps:
            if eng != eng_name:
                continue
            for w in waits:
                weng, val = self.events[w]
                if weng == eng_name:
                    assert val <= passed, f"forward same-engine wait {w}"
                if last.get(weng, 0) < val:
                    eng_obj.wait_ge(sems[weng], val)
                    last[weng] = val
            instr = fn(eng_obj)
            if inc is not None:
                amt = 16 if eng in ("sync", "gpsimd") else 1
                instr.then_inc(sems[eng], amt)
                passed += amt


def build_program():
    nc = bass.Bass("TRN2", target_bir_lowering=False)

    xin = nc.dram_tensor("x", [B_PER, T, D], F32, kind="ExternalInput").ap()
    wqkv_d = nc.dram_tensor("wqkv", [D, 3 * D], F32, kind="ExternalInput").ap()
    wproj_d = nc.dram_tensor("wproj", [D, D], F32, kind="ExternalInput").ap()
    wrT_d = nc.dram_tensor("wrT", [D, D], F32, kind="ExternalInput").ap()
    wkT_d = nc.dram_tensor("wkT", [D, D], F32, kind="ExternalInput").ap()
    wvT_d = nc.dram_tensor("wvT", [D, D], F32, kind="ExternalInput").ap()
    mur_d = nc.dram_tensor("mur", [128, NC], F32, kind="ExternalInput").ap()
    muk_d = nc.dram_tensor("muk", [128, NC], F32, kind="ExternalInput").ap()
    cos_d = nc.dram_tensor("costab", [128, T], F32, kind="ExternalInput").ap()
    sin_d = nc.dram_tensor("sintab", [128, T], F32, kind="ExternalInput").ap()
    identb_d = nc.dram_tensor("identb", [128, 128], BF16, kind="ExternalInput").ap()
    identf_d = nc.dram_tensor("identf", [128, 128], F32, kind="ExternalInput").ap()
    ones_d = nc.dram_tensor("onesb", [128, 64], BF16, kind="ExternalInput").ap()
    out_d = nc.dram_tensor("out", [B_PER, T, D], F32, kind="ExternalOutput").ap()
    xspill = nc.dram_tensor("xspill", [B_PER, T, D], F32, kind="Internal").ap()

    AW = 53100
    arena = nc.alloc_sbuf_tensor("arena", [128, AW], F32)
    arena_ap = arena.ap()
    cur = [0]

    def alloc(words, align=16):
        cur[0] = (cur[0] + align - 1) // align * align
        off = cur[0]
        cur[0] += words
        assert cur[0] <= AW, f"arena overflow: {cur[0]} > {AW}"
        return off

    def fview(off, words):
        return arena_ap[:, off:off + words]

    def bview(off, n_bf16):
        return arena_ap[:, off:off + n_bf16 // 2].bitcast(BF16)

    o_cos = alloc(T)
    o_sin = alloc(T)
    o_identb = alloc(64)
    o_identf = alloc(128)
    o_ones = alloc(32)
    o_mur = alloc(NC)
    o_muk = alloc(NC)
    o_eps = alloc(16)
    o_stats = alloc(16)
    o_rs = alloc(16)
    o_rd = alloc(512)
    o_w = alloc(12288)
    o_wr, o_wk, o_wv = o_w, o_w + 4096, o_w + 8192
    YW = YPAD + T + 2
    o_y = alloc(NC * YW // 2)
    o_x = alloc(NT * D)
    o_strip = o_x
    o_qrot = o_x + 8192
    o_krot = o_x + 9216
    o_vg = o_x + 10240
    SCR = 11264
    o_s = alloc(SCR)
    o_xr = o_s
    o_xk = o_s + 2048
    o_kact = o_s + 4096
    o_krelu = o_s + 6144
    o_rsm = o_s + 7168
    o_prod = o_s + 8192
    o_dtmp = o_s + 10240
    o_ynat = o_s + 2048
    o_attnout = o_s
    o_rope1 = o_s + 8192
    o_rope2 = o_s + 8704
    o_lnst = o_s + 9216
    o_sqscr = o_s + 10240

    psF = nc.alloc_psum_tensor("psF", [128, 3072], F32).ap()
    psB = nc.alloc_psum_tensor("psB", [128, 2048], BF16).ap()

    def bank(i, w=512):
        return psF[:, i * 512:i * 512 + w]

    P = Plan()

    def dma(q, dst, src, wait=(), inc=None):
        P.op(q, lambda e, dst=dst, src=src: e.dma_start(out=dst, in_=src),
             wait=wait, inc=inc)

    COS = fview(o_cos, T)
    SIN = fview(o_sin, T)
    IDB = bview(o_identb, 128)
    IDF = fview(o_identf, 128)
    ONES = bview(o_ones, 64)
    MUR = fview(o_mur, NC)
    MUK = fview(o_muk, NC)
    EPS = fview(o_eps, 1)
    RD = fview(o_rd, 512)
    WQKV = bview(o_w, 24576).rearrange("p (c n) -> p c n", c=NC)
    WPROJ = bview(o_w, 8192).rearrange("p (c n) -> p c n", c=NC)
    WR = bview(o_wr, 8192).rearrange("p (c n) -> p c n", c=NC)
    WK = bview(o_wk, 8192).rearrange("p (c n) -> p c n", c=NC)
    WV = bview(o_wv, 8192).rearrange("p (c n) -> p c n", c=NC)
    Y = bview(o_y, NC * YW).rearrange("p (r n) -> p r n", r=NC)
    XN = fview(o_x, NT * D).rearrange("p (t d) -> p t d", t=NT)
    STRIP = bview(o_strip, 16384).rearrange("p (h k n) -> p h k n", h=2, k=NT)
    QROT = bview(o_qrot, 2048)
    KROT = bview(o_krot, 2048)
    VG = bview(o_vg, 4096).rearrange("p (t n) -> p t n", t=NT)
    XR = bview(o_xr, 4096).rearrange("p (c n) -> p c n", c=NC)
    XK = bview(o_xk, 4096).rearrange("p (c n) -> p c n", c=NC)
    KACT = bview(o_kact, 4096).rearrange("p (c n) -> p c n", c=NC)
    KRELU = fview(o_krelu, 1024).rearrange("p (a n) -> p a n", a=2)
    RSM = fview(o_rsm, 1024).rearrange("p (a n) -> p a n", a=2)
    PROD = fview(o_prod, 2048).rearrange("p (a n) -> p a n", a=4)
    DTMP = fview(o_dtmp, 512)
    YNAT = bview(o_ynat, 4096).rearrange("p (a n) -> p a n", a=4)
    ATTO = bview(o_attnout, 16384).rearrange("p (r n) -> p r n", r=NC)
    ROPE1 = fview(o_rope1, 512)
    ROPE2 = fview(o_rope2, 512)
    S0 = fview(o_lnst + 0, 16)
    S1 = fview(o_lnst + 16, 16)
    NEGMEAN = fview(o_lnst + 32, 16)
    EX2 = fview(o_lnst + 48, 16)
    MSQ = fview(o_lnst + 64, 16)
    VAR = fview(o_lnst + 80, 16)
    SD = fview(o_lnst + 96, 16)
    RSTD = fview(o_lnst + 112, 16)
    NMB = fview(o_lnst + 128, 16)
    SQSCR = fview(o_sqscr, 1024)

    uid = [0]

    def ev(tag):
        uid[0] += 1
        return f"{tag}.{uid[0]}"

    dma("sync", COS, cos_d, inc="c_cos")
    dma("sync", SIN, sin_d, inc="c_sin")
    dma("sync", IDB, identb_d, inc="c_idb")
    dma("sync", IDF, identf_d, inc="c_idf")
    dma("sync", ONES, ones_d, inc="c_ones")
    dma("sync", MUR, mur_d, inc="c_mur")
    dma("sync", MUK, muk_d, inc="c_muk")
    P.op("vector", lambda e: e.memset(EPS, 1e-5), inc="c_eps")
    P.op("vector", lambda e: e.memset(Y[:, :, 0:YPAD], 0.0), inc="c_y0")

    def load_ffn_w(tag, wait=()):
        evs = []
        for nm, off, src in (("r", o_wr, wrT_d), ("k", o_wk, wkT_d),
                             ("v", o_wv, wvT_d)):
            e_name = f"wf{nm}.{tag}"
            dma("gpsimd", bview(off, 8192).rearrange("p (c n) -> p c n", c=NC),
                src.rearrange("(c p) n -> p c n", p=128), wait=wait, inc=e_name)
            evs.append(e_name)
        return evs

    def load_wqkv(tag, wait=()):
        dma("gpsimd", bview(o_w, 24576).rearrange("p (c n) -> p c n", c=NC),
            wqkv_d.rearrange("(c p) n -> p c n", p=128), wait=wait,
            inc=f"wqkv.{tag}")
        return [f"wqkv.{tag}"]

    def load_wproj(tag, wait=()):
        dma("gpsimd", bview(o_w, 8192).rearrange("p (c n) -> p c n", c=NC),
            wproj_d.rearrange("(c p) n -> p c n", p=128), wait=wait,
            inc=f"wproj.{tag}")
        return [f"wproj.{tag}"]

    def load_x(b, tag, wait=()):
        dma("sync", XN,
            xin[b].rearrange("(t p) d -> p t d", p=128), wait=wait,
            inc=f"x.{tag}")
        return f"x.{tag}"

    def ln_phase(b, sub, x_deps):
        tag = f"ln{b}{sub}"
        rs_ev = sq_ev = None
        for t in range(NT):
            rs_ev = ev(f"{tag}.rs")
            P.op("vector", lambda e, t=t: e.reduce_sum(
                S0[:, t:t + 1], XN[:, t, :], axis=AX.X),
                wait=list(x_deps), inc=rs_ev)
            sq_ev = ev(f"{tag}.sq")
            P.op("scalar", lambda e, t=t: e.activation(
                out=SQSCR, in_=XN[:, t, :], func=AF.Square,
                accum_out=S1[:, t:t + 1]),
                wait=list(x_deps), inc=sq_ev)
        e_nm = ev(f"{tag}.nm")
        P.op("vector", lambda e: e.tensor_scalar(
            NEGMEAN, S0, -1.0 / D, None, ALU.mult), wait=[rs_ev], inc=e_nm)
        e_e2 = ev(f"{tag}.e2")
        P.op("vector", lambda e: e.tensor_scalar(
            EX2, S1, 1.0 / D, None, ALU.mult), wait=[sq_ev], inc=e_e2)
        e_ms = ev(f"{tag}.ms")
        P.op("vector", lambda e: e.tensor_tensor(
            MSQ, NEGMEAN, NEGMEAN, ALU.mult), wait=[e_nm], inc=e_ms)
        e_va = ev(f"{tag}.va")
        P.op("vector", lambda e: e.tensor_tensor(
            VAR, EX2, MSQ, ALU.subtract), wait=[e_e2, e_ms], inc=e_va)
        e_sd = ev(f"{tag}.sd")
        P.op("scalar", lambda e: e.activation(
            out=SD, in_=VAR, func=AF.Sqrt, bias=EPS),
            wait=[e_va, "c_eps"], inc=e_sd)
        e_rstd = ev(f"{tag}.rstd")
        P.op("vector", lambda e: e.reciprocal(RSTD, SD), wait=[e_sd],
             inc=e_rstd)
        e_nb = ev(f"{tag}.nb")
        P.op("vector", lambda e: e.tensor_tensor(
            NMB, NEGMEAN, RSTD, ALU.mult), wait=[e_rstd], inc=e_nb)
        act_ev = [None] * NT
        for t in range(NT):
            e_act = f"{tag}.act{t}"
            w = [e_nb, e_rstd]
            if t >= 4:
                w.append(f"{tag}.tp{t // 4 - 1}.{NC-1}")
            P.op("scalar", lambda e, t=t, slot=t % 4: e.activation(
                out=YNAT[:, slot, :], in_=XN[:, t, :], func=AF.Identity,
                bias=NMB[:, t:t + 1], scale=RSTD[:, t:t + 1]),
                wait=w, inc=e_act)
            act_ev[t] = e_act
            if t % 4 == 3:
                g = t // 4
                for r in range(NC):
                    e_tp = f"{tag}.tp{g}.{r}"
                    def mk_tp(r=r):
                        def f(e):
                            last = None
                            for j in range(4):
                                last = e.transpose(
                                    psB[:, (r % 2) * 1024 + j * 128:
                                        (r % 2) * 1024 + j * 128 + 128],
                                    YNAT[:, j, r * 128:(r + 1) * 128], IDB)
                            return last
                        return f
                    w = [act_ev[g * 4 + j] for j in range(4)] + ["c_idb"]
                    if g > 0 or r >= 2:
                        pg = g if r >= 2 else g - 1
                        pr = r - 2 if r >= 2 else r + NC - 2
                        w.append(f"{tag}.evc{pg}.{pr}")
                    P.op("tensor", mk_tp(), wait=w, inc=e_tp)
                    P.op("vector", lambda e, r=r, g=g: e.tensor_copy(
                        Y[:, r, YPAD + g * 512: YPAD + g * 512 + 512],
                        psB[:, (r % 2) * 1024:(r % 2) * 1024 + 512]),
                        wait=[e_tp, "c_y0"], inc=f"{tag}.evc{g}.{r}")
        return [f"{tag}.evc3.{r}" for r in range(NC)]

    def ffn_phase(b, sub, w_evs, y_done, x_deps):
        tag = f"f{b}{sub}"
        prev_chunk = []
        adds = []
        for c in range(NQC):
            ybase = YPAD + c * QC
            mix_ev = []
            for r in range(NC):
                y_c = Y[:, r, ybase:ybase + QC]
                y_p = Y[:, r, ybase - 1:ybase - 1 + QC]
                e_d = ev(f"{tag}.d")
                P.op("vector", lambda e, y_c=y_c, y_p=y_p: e.tensor_tensor(
                    DTMP, y_c, y_p, ALU.subtract),
                    wait=list(y_done) + prev_chunk, inc=e_d)
                e_xr = ev(f"{tag}.xr")
                P.op("vector", lambda e, r=r, y_p=y_p: e.scalar_tensor_tensor(
                    XR[:, r, :], DTMP, MUR[:, r:r + 1], y_p, ALU.mult, ALU.add),
                    wait=["c_mur", e_d], inc=e_xr)
                e_xk = ev(f"{tag}.xk")
                P.op("vector", lambda e, r=r, y_p=y_p: e.scalar_tensor_tensor(
                    XK[:, r, :], DTMP, MUK[:, r:r + 1], y_p, ALU.mult, ALU.add),
                    wait=["c_muk", e_d], inc=e_xk)
                mix_ev += [e_xr, e_xk]
            kact_ev = []
            for mt in range(NC):
                bk = mt % 2
                e_mm = ev(f"{tag}.kmm")
                def mk_k(mt=mt, bk=bk):
                    def f(e):
                        last = None
                        for cc in range(NC):
                            last = e.matmul(bank(bk),
                                            WK[:, cc, mt * 128:(mt + 1) * 128],
                                            XK[:, cc, :], start=(cc == 0),
                                            stop=(cc == NC - 1))
                        return last
                    return f
                w = mix_ev + list(w_evs)
                if mt >= 2:
                    w.append(f"{tag}.krl{c}.{mt-2}")
                P.op("tensor", mk_k(), wait=w, inc=e_mm)
                e_rl = f"{tag}.krl{c}.{mt}"
                P.op("scalar", lambda e, bk=bk: e.activation(
                    out=KRELU[:, bk, :], in_=bank(bk), func=AF.Relu),
                    wait=[e_mm], inc=e_rl)
                e_sq = ev(f"{tag}.ksq")
                P.op("scalar", lambda e, mt=mt, bk=bk: e.activation(
                    out=KACT[:, mt, :], in_=KRELU[:, bk, :], func=AF.Square),
                    wait=[e_rl], inc=e_sq)
                kact_ev.append(e_sq)
            for half in range(2):
                prod_ev = []
                for i in range(4):
                    mt = half * 4 + i
                    bkr = 2 + (i % 2)
                    bkv = 4 + (i % 2)
                    e_r = ev(f"{tag}.rmm")
                    def mk_r(mt=mt, bkr=bkr):
                        def f(e):
                            last = None
                            for cc in range(NC):
                                last = e.matmul(bank(bkr),
                                                WR[:, cc, mt * 128:(mt + 1) * 128],
                                                XR[:, cc, :], start=(cc == 0),
                                                stop=(cc == NC - 1))
                            return last
                        return f
                    w = mix_ev + list(w_evs)
                    if mt >= 2:
                        w.append(f"{tag}.sg{c}.{mt-2}")
                    P.op("tensor", mk_r(), wait=w, inc=e_r)
                    e_sg = f"{tag}.sg{c}.{mt}"
                    w = [e_r]
                    if mt >= 2:
                        w.append(f"{tag}.pr{c}.{mt-2}")
                    P.op("scalar", lambda e, i=i, bkr=bkr: e.activation(
                        out=RSM[:, i % 2, :], in_=bank(bkr), func=AF.Sigmoid),
                        wait=w, inc=e_sg)
                    e_v = ev(f"{tag}.vmm")
                    def mk_v(mt=mt, bkv=bkv):
                        def f(e):
                            last = None
                            for cc in range(NC):
                                last = e.matmul(bank(bkv),
                                                WV[:, cc, mt * 128:(mt + 1) * 128],
                                                KACT[:, cc, :], start=(cc == 0),
                                                stop=(cc == NC - 1))
                            return last
                        return f
                    w = kact_ev + list(w_evs)
                    if mt >= 2:
                        w.append(f"{tag}.pr{c}.{mt-2}")
                    P.op("tensor", mk_v(), wait=w, inc=e_v)
                    e_pr = f"{tag}.pr{c}.{mt}"
                    P.op("vector", lambda e, i=i, bkv=bkv: e.tensor_tensor(
                        PROD[:, i, :], RSM[:, i % 2, :], bank(bkv), ALU.mult),
                        wait=[e_sg, e_v], inc=e_pr)
                    prod_ev.append(e_pr)
                for tb in range(4):
                    bkT = tb % 2
                    e_tp = f"{tag}.tp{c}.{half}.{tb}"
                    def mk_tp(tb=tb, bkT=bkT):
                        def f(e):
                            last = None
                            for i in range(4):
                                last = e.transpose(
                                    psF[:, bkT * 512 + i * 128:
                                        bkT * 512 + i * 128 + 128],
                                    PROD[:, i, tb * 128:(tb + 1) * 128], IDF)
                            return last
                        return f
                    w = prod_ev + ["c_idf"]
                    if tb >= 2:
                        w.append(f"{tag}.ad{c}.{half}.{tb-2}")
                    elif half == 1:
                        w.append(f"{tag}.ad{c}.0.{tb+2}")
                    P.op("tensor", mk_tp(), wait=w, inc=e_tp)
                    e_ad = f"{tag}.ad{c}.{half}.{tb}"
                    t_idx = c * 4 + tb
                    P.op("vector", lambda e, t_idx=t_idx, half=half, bkT=bkT:
                         e.scalar_tensor_tensor(
                             XN[:, t_idx, half * 512:half * 512 + 512],
                             bank(bkT), 0.5,
                             XN[:, t_idx, half * 512:half * 512 + 512],
                             ALU.mult, ALU.add),
                         wait=[e_tp] + list(x_deps), inc=e_ad)
                    adds.append(e_ad)
            prev_chunk = [adds[-1]]
        return adds

    def attn_phase(b, y_done, wq_evs, spill_done):
        tag = f"a{b}"
        no_evs = []
        prev_group_pv = []
        last_exp = {0: None, 1: None}
        last_no = [None]
        last_rc = [None]
        gate = list(y_done) + [spill_done]
        for g in range(4):
            vev = []
            for nt in range(NT):
                bk = nt % 2
                e_v = ev(f"{tag}.vmm")
                def mk_v(nt=nt, g=g, bk=bk):
                    def f(e):
                        last = None
                        for cc in range(NC):
                            last = e.matmul(
                                bank(bk, 256),
                                Y[:, cc, YPAD + nt * 128:YPAD + nt * 128 + 128],
                                WQKV[:, cc, 2048 + g * 256:2048 + (g + 1) * 256],
                                start=(cc == 0), stop=(cc == NC - 1))
                        return last
                    return f
                w = gate + list(wq_evs)
                if nt >= 2:
                    w.append(f"{tag}.ve{g}.{nt-2}")
                elif g > 0:
                    w.append(f"{tag}.ve{g-1}.{NT-2+nt}")
                    w += prev_group_pv
                P.op("tensor", mk_v(), wait=w, inc=e_v)
                P.op("vector", lambda e, nt=nt, bk=bk: e.tensor_copy(
                    VG[:, nt, :], bank(bk, 256)), wait=[e_v],
                    inc=f"{tag}.ve{g}.{nt}")
                vev.append(f"{tag}.ve{g}.{nt}")
            group_pv = []
            for ph in range(2):
                p = g * 2 + ph
                rope_done = []
                for which, dst, colbase in (("q", QROT, p * 128),
                                            ("k", KROT, 1024 + p * 128)):
                    for ncq in range(NQC):
                        bk = (2 if which == "q" else 4) + (ncq % 2)
                        e_mm = ev(f"{tag}.qk")
                        def mk_qk(ncq=ncq, colbase=colbase, bk=bk):
                            def f(e):
                                last = None
                                for cc in range(NC):
                                    last = e.matmul(
                                        bank(bk),
                                        WQKV[:, cc, colbase:colbase + 128],
                                        Y[:, cc, YPAD + ncq * QC:
                                          YPAD + ncq * QC + QC],
                                        start=(cc == 0), stop=(cc == NC - 1))
                                return last
                            return f
                        w = gate + list(wq_evs)
                        if ncq >= 2:
                            w.append(f"{tag}.rp{p}{which}{ncq-2}")
                        elif which == "q":
                            w.append(last_exp[1])
                        else:
                            w.append(last_no[0])
                            w.append(last_rc[0])
                        P.op("tensor", mk_qk(), wait=w, inc=e_mm)
                        cs = COS[:, ncq * QC:(ncq + 1) * QC]
                        sn = SIN[:, ncq * QC:(ncq + 1) * QC]
                        pb = bank(bk)
                        e_r1 = ev(f"{tag}.r1")
                        P.op("vector", lambda e, pb=pb, cs=cs: e.tensor_tensor(
                            ROPE1, pb, cs, ALU.mult),
                            wait=[e_mm, "c_cos"], inc=e_r1)
                        e_r2 = ev(f"{tag}.r2")
                        def mk_r2(pb=pb, sn=sn):
                            def f(e):
                                last = None
                                for blk in range(4):
                                    ob = (blk // 2) * 64 + (blk % 2) * 32
                                    sb = (blk // 2) * 64 + (1 - blk % 2) * 32
                                    last = e.tensor_tensor(
                                        ROPE2[ob:ob + 32, :],
                                        pb[sb:sb + 32, :],
                                        sn[ob:ob + 32, :], ALU.mult)
                                return last
                            return f
                        P.op("vector", mk_r2(), wait=[e_mm, "c_sin"], inc=e_r2)
                        e_rp = f"{tag}.rp{p}{which}{ncq}"
                        P.op("vector", lambda e, dst=dst, ncq=ncq: e.tensor_tensor(
                            dst[:, ncq * QC:(ncq + 1) * QC], ROPE1, ROPE2,
                            ALU.add), wait=[e_r1, e_r2], inc=e_rp)
                        rope_done.append(e_rp)
                for qc in range(NQC):
                    exp_evs = []
                    for kb2 in range(8):
                        sj_evs = []
                        for j in range(2):
                            e_s = ev(f"{tag}.sc")
                            def mk_s(j=j, kb2=kb2, qc=qc):
                                def f(e):
                                    last = None
                                    kb = kb2 * 2 + j
                                    for h in range(2):
                                        last = e.matmul(
                                            bank(2 * h + j),
                                            KROT[h * 64:(h + 1) * 64,
                                                 kb * 128:(kb + 1) * 128],
                                            QROT[h * 64:(h + 1) * 64,
                                                 qc * QC:(qc + 1) * QC],
                                            start=True, stop=True,
                                            tile_position=(h * 64, 0))
                                    return last
                                return f
                            w = list(rope_done) + [last_exp[0], last_exp[1]]
                            P.op("tensor", mk_s(), wait=w, inc=e_s)
                            sj_evs.append(e_s)
                        for h in range(2):
                            e_x = f"{tag}.ex{p}.{qc}.{h}.{kb2}"
                            P.op("scalar", lambda e, h=h, kb2=kb2:
                                 e.activation(
                                     out=STRIP[:, h, kb2 * 2:kb2 * 2 + 2, :],
                                     in_=psF[:, h * 1024:h * 1024 + 1024],
                                     func=AF.Exp, scale=0.125),
                                 wait=list(sj_evs), inc=e_x)
                            last_exp[h] = e_x
                            exp_evs.append(e_x)
                    e_pv1 = ev(f"{tag}.pv1")
                    def mk_pv1(ph=ph, qc=qc):
                        def f(e):
                            last = None
                            for kt in range(NT):
                                last = e.matmul(
                                    bank(4)[0:64, :],
                                    VG[:, kt, (ph * 2) * 64:
                                       (ph * 2 + 1) * 64],
                                    STRIP[:, 0, kt, :],
                                    start=(kt == 0), stop=(kt == NT - 1),
                                    tile_position=(0, 0))
                                last = e.matmul(
                                    bank(5)[64:128, :],
                                    ONES, STRIP[:, 1, kt, :],
                                    start=(kt == 0), stop=(kt == NT - 1),
                                    tile_position=(0, 64))
                            return last
                        return f
                    w = exp_evs + vev + ["c_ones"]
                    w.append(last_no[0])
                    w.append(last_rc[0])
                    P.op("tensor", mk_pv1(), wait=w, inc=e_pv1)
                    e_pv2 = ev(f"{tag}.pv2")
                    def mk_pv2(ph=ph, qc=qc):
                        def f(e):
                            last = None
                            for kt in range(NT):
                                last = e.matmul(
                                    bank(5)[0:64, :],
                                    ONES, STRIP[:, 0, kt, :],
                                    start=(kt == 0), stop=(kt == NT - 1),
                                    tile_position=(0, 0))
                                last = e.matmul(
                                    bank(4)[64:128, :],
                                    VG[:, kt, (ph * 2 + 1) * 64:
                                       (ph * 2 + 2) * 64],
                                    STRIP[:, 1, kt, :],
                                    start=(kt == 0), stop=(kt == NT - 1),
                                    tile_position=(0, 64))
                            return last
                        return f
                    P.op("tensor", mk_pv2(),
                         wait=exp_evs + vev + ["c_ones", e_pv1], inc=e_pv2)
                    e_rc = f"{tag}.rc{p}.{qc}"
                    P.op("vector", lambda e: e.reciprocal(RD, bank(5)),
                         wait=[e_pv1, e_pv2], inc=e_rc)
                    last_rc[0] = e_rc
                    e_no = f"{tag}.no{p}.{qc}"
                    P.op("vector", lambda e, p=p, qc=qc: e.tensor_tensor(
                        ATTO[:, p, qc * QC:(qc + 1) * QC], bank(4), RD,
                        ALU.mult), wait=[e_rc, e_pv2], inc=e_no)
                    last_no[0] = e_no
                    no_evs.append(e_no)
                group_pv.append(f"{tag}.no{p}.{NQC-1}")
            prev_group_pv = group_pv
        wproj_ev = load_wproj(tag, wait=[no_evs[-1]])
        xrel = []
        for t in range(NT):
            dma("sync", XN[:, t, :],
                xspill[b].rearrange("(t p) d -> p t d", p=128)[:, t, :],
                wait=[no_evs[-1]] if t == 0 else [], inc=f"{tag}.xr{t}")
            xrel.append(f"{tag}.xr{t}")
        adds = []
        for qc in range(NQC):
            for mt in range(NC):
                bk = mt % 2
                e_mm = ev(f"{tag}.pj")
                def mk_pj(mt=mt, qc=qc, bk=bk):
                    def f(e):
                        last = None
                        for cc in range(NC):
                            last = e.matmul(
                                bank(bk), WPROJ[:, cc, mt * 128:(mt + 1) * 128],
                                ATTO[:, cc, qc * QC:(qc + 1) * QC],
                                start=(cc == 0), stop=(cc == NC - 1))
                        return last
                    return f
                w = list(wproj_ev) + [no_evs[-1]]
                if mt >= 2:
                    w.append(f"{tag}.pc{qc}.{mt-2}")
                elif qc > 0:
                    w.append(f"{tag}.pc{qc-1}.{mt+6}")
                P.op("tensor", mk_pj(), wait=w, inc=e_mm)
                P.op("vector", lambda e, slot=mt % 4, bk=bk: e.tensor_copy(
                    PROD[:, slot, :], bank(bk)), wait=[e_mm],
                    inc=f"{tag}.pc{qc}.{mt}")
                if mt % 4 == 3:
                    half = mt // 4
                    for tb in range(4):
                        bkT = 2 + (tb % 2)
                        e_tp = f"{tag}.ptp{qc}.{half}.{tb}"
                        def mk_tp(tb=tb, bkT=bkT):
                            def f(e):
                                last = None
                                for i in range(4):
                                    last = e.transpose(
                                        psF[:, bkT * 512 + i * 128:
                                            bkT * 512 + i * 128 + 128],
                                        PROD[:, i, tb * 128:(tb + 1) * 128], IDF)
                                return last
                            return f
                        w = [f"{tag}.pc{qc}.{half*4+j}" for j in range(4)]
                        w.append("c_idf")
                        if tb >= 2:
                            w.append(f"{tag}.pa{qc}.{half}.{tb-2}")
                        elif half == 1:
                            w.append(f"{tag}.pa{qc}.0.{tb+2}")
                        elif qc > 0:
                            w.append(f"{tag}.pa{qc-1}.1.{tb+2}")
                        P.op("tensor", mk_tp(), wait=w, inc=e_tp)
                        e_ad = f"{tag}.pa{qc}.{half}.{tb}"
                        t_idx = qc * 4 + tb
                        P.op("vector", lambda e, t_idx=t_idx, half=half,
                             bkT=bkT: e.tensor_tensor(
                                 XN[:, t_idx, half * 512:half * 512 + 512],
                                 bank(bkT),
                                 XN[:, t_idx, half * 512:half * 512 + 512],
                                 ALU.add), wait=[e_tp, xrel[t_idx]], inc=e_ad)
                        adds.append(e_ad)
        return adds

    ffn_w_ev = load_ffn_w("init")
    out_evs = []
    prev_out = None
    wf = ffn_w_ev
    for b in range(B_PER):
        xev = load_x(b, f"b{b}", wait=[prev_out] if prev_out else [])
        y1 = ln_phase(b, 1, [xev])
        f1 = ffn_phase(b, 1, wf, y1, [xev])
        spill_evs = []
        for c in range(NQC):
            dma("sync",
                xspill[b].rearrange("(g p) d -> p g d", p=128)[:, c * 4:c * 4 + 4, :],
                XN[:, c * 4:c * 4 + 4, :],
                wait=[f1[c * 8 + 7]], inc=f"spill{b}.{c}")
            spill_evs.append(f"spill{b}.{c}")
        y2 = ln_phase(b, 2, [f1[-1]])
        wq = load_wqkv(f"b{b}", wait=[f1[-1]])
        a2 = attn_phase(b, y2, wq, spill_evs[-1])
        wf = load_ffn_w(f"s3b{b}", wait=[a2[-1]])
        y3 = ln_phase(b, 3, [a2[-1]])
        f3 = ffn_phase(b, 3, wf, y3, a2[-8:])
        for c in range(NQC):
            dma("sync",
                out_d[b].rearrange("(g p) d -> p g d", p=128)[:, c * 4:c * 4 + 4, :],
                XN[:, c * 4:c * 4 + 4, :],
                wait=[f3[c * 8 + 7]], inc=f"out{b}.{c}")
        out_evs.append(f"out{b}.{NQC-1}")
        prev_out = f"out{b}.{NQC-1}"

    P.op("sync", lambda e: e.nop(), wait=out_evs)

    from contextlib import ExitStack
    with ExitStack() as ctx:
        e = ctx.enter_context
        sems = {name: e(nc.semaphore(f"sem_{name}")) for name in ENGINES}
        blk = e(nc.Block())

        @blk.tensor
        def _(te):
            P.emit("tensor", te, sems)

        @blk.vector
        def _(ve):
            P.emit("vector", ve, sems)

        @blk.scalar
        def _(sc):
            P.emit("scalar", sc, sems)

        @blk.sync
        def _(sy):
            P.emit("sync", sy, sems)

        @blk.gpsimd
        def _(gp):
            P.emit("gpsimd", gp, sems)

    return nc


_NC_CACHE = None


def kernel(L0_attn_x, L0_attn_w_qkv, L0_attn_w_proj, L0_ffn_W_r, L0_ffn_W_k,
           L0_ffn_W_v, L0_ffn_mu_r, L0_ffn_mu_k):
    global _NC_CACHE
    x = np.ascontiguousarray(np.asarray(L0_attn_x, dtype=np.float32))
    wqkv = np.asarray(L0_attn_w_qkv, dtype=np.float32)
    wproj = np.ascontiguousarray(np.asarray(L0_attn_w_proj, dtype=np.float32))
    wrT = np.ascontiguousarray(np.asarray(L0_ffn_W_r, dtype=np.float32).T)
    wkT = np.ascontiguousarray(np.asarray(L0_ffn_W_k, dtype=np.float32).T)
    wvT = np.ascontiguousarray(np.asarray(L0_ffn_W_v, dtype=np.float32).T)
    mur = np.ascontiguousarray(
        np.asarray(L0_ffn_mu_r, np.float32).reshape(NC, 128).T)
    muk = np.ascontiguousarray(
        np.asarray(L0_ffn_mu_k, np.float32).reshape(NC, 128).T)

    src = np.empty(HD, np.int64)
    src[0:32] = np.arange(0, 64, 2)
    src[32:64] = np.arange(1, 64, 2)
    colperm = np.concatenate([h * HD + src for h in range(H)])
    wqkv_p = np.ascontiguousarray(np.concatenate(
        [wqkv[:, 0:D][:, colperm], wqkv[:, D:2 * D][:, colperm],
         wqkv[:, 2 * D:3 * D]], axis=1))

    inv = 1.0 / 10000.0 ** (np.arange(0, HD, 2, dtype=np.float64) / HD)
    ang = np.arange(T, dtype=np.float64)[None, :] * inv[:, None]
    cos32 = np.cos(ang).astype(np.float32)
    sin32 = np.sin(ang).astype(np.float32)
    costab = np.ascontiguousarray(np.tile(np.concatenate([cos32, cos32], 0),
                                          (2, 1)))
    sintab = np.ascontiguousarray(np.tile(np.concatenate([-sin32, sin32], 0),
                                          (2, 1)))
    identb = np.eye(128, dtype=ml_dtypes.bfloat16)
    identf = np.eye(128, dtype=np.float32)
    onesb = np.ones((128, 64), dtype=ml_dtypes.bfloat16)

    if _NC_CACHE is None:
        _NC_CACHE = build_program()
    nc = _NC_CACHE

    in_maps = []
    for c in range(NCORES):
        in_maps.append({
            "x": np.ascontiguousarray(x[c * B_PER:(c + 1) * B_PER]),
            "wqkv": wqkv_p, "wproj": wproj,
            "wrT": wrT, "wkT": wkT, "wvT": wvT,
            "mur": mur, "muk": muk,
            "costab": costab, "sintab": sintab,
            "identb": identb, "identf": identf, "onesb": onesb,
        })
    res = run_bass_kernel_spmd(nc, in_maps, core_ids=list(range(NCORES)))
    outs = [np.asarray(res.results[c]["out"], dtype=np.float32)
            for c in range(NCORES)]
    return np.concatenate(outs, axis=0)


# version 05
# speedup vs baseline: 1.1031x; 1.1031x over previous
import numpy as np
import ml_dtypes
import concourse.bass as bass
import concourse.mybir as mybir
from concourse.bass_utils import run_bass_kernel_spmd

F32 = mybir.dt.float32
BF16 = mybir.dt.bfloat16
AF = mybir.ActivationFunctionType
ALU = mybir.AluOpType
AX = mybir.AxisListType

NCORES = 8
B_PER = 2
T = 2048
D = 1024
H = 16
HD = 64
NT = T // 128
NC = D // 128
NQC = 4
QC = 512
YPAD = 2

ENGINES = ("tensor", "vector", "scalar", "sync", "gpsimd")


class Plan:

    def __init__(self):
        self.steps = []
        self.events = {}
        self.counts = {e: 0 for e in ENGINES}

    def op(self, eng, fn, wait=(), inc=None):
        if inc is not None:
            amt = 16 if eng in ("sync", "gpsimd") else 1
            self.counts[eng] += amt
            assert inc not in self.events, f"duplicate event {inc}"
            self.events[inc] = (eng, self.counts[eng])
        self.steps.append((eng, fn, tuple(w for w in wait if w), inc))

    def emit(self, eng_name, eng_obj, sems):
        last = {}
        passed = 0
        for eng, fn, waits, inc in self.steps:
            if eng != eng_name:
                continue
            for w in waits:
                weng, val = self.events[w]
                if weng == eng_name:
                    assert val <= passed, f"forward same-engine wait {w}"
                if last.get(weng, 0) < val:
                    eng_obj.wait_ge(sems[weng], val)
                    last[weng] = val
            instr = fn(eng_obj)
            if inc is not None:
                amt = 16 if eng in ("sync", "gpsimd") else 1
                instr.then_inc(sems[eng], amt)
                passed += amt


def build_program():
    nc = bass.Bass("TRN2", target_bir_lowering=False)

    xin = nc.dram_tensor("x", [B_PER, T, D], F32, kind="ExternalInput").ap()
    wqkv_d = nc.dram_tensor("wqkv", [D, 3 * D], F32, kind="ExternalInput").ap()
    wproj_d = nc.dram_tensor("wproj", [D, D], F32, kind="ExternalInput").ap()
    wrT_d = nc.dram_tensor("wrT", [D, D], F32, kind="ExternalInput").ap()
    wkT_d = nc.dram_tensor("wkT", [D, D], F32, kind="ExternalInput").ap()
    wvT_d = nc.dram_tensor("wvT", [D, D], F32, kind="ExternalInput").ap()
    mur_d = nc.dram_tensor("mur", [128, NC], F32, kind="ExternalInput").ap()
    muk_d = nc.dram_tensor("muk", [128, NC], F32, kind="ExternalInput").ap()
    cos_d = nc.dram_tensor("costab", [128, T], F32, kind="ExternalInput").ap()
    sin_d = nc.dram_tensor("sintab", [128, T], F32, kind="ExternalInput").ap()
    identb_d = nc.dram_tensor("identb", [128, 128], BF16, kind="ExternalInput").ap()
    identf_d = nc.dram_tensor("identf", [128, 128], F32, kind="ExternalInput").ap()
    ones_d = nc.dram_tensor("onesb", [128, 64], BF16, kind="ExternalInput").ap()
    out_d = nc.dram_tensor("out", [B_PER, T, D], F32, kind="ExternalOutput").ap()
    xspill = nc.dram_tensor("xspill", [B_PER, T, D], F32, kind="Internal").ap()

    AW = 53100
    arena = nc.alloc_sbuf_tensor("arena", [128, AW], F32)
    arena_ap = arena.ap()
    cur = [0]

    def alloc(words, align=16):
        cur[0] = (cur[0] + align - 1) // align * align
        off = cur[0]
        cur[0] += words
        assert cur[0] <= AW, f"arena overflow: {cur[0]} > {AW}"
        return off

    def fview(off, words):
        return arena_ap[:, off:off + words]

    def bview(off, n_bf16):
        return arena_ap[:, off:off + n_bf16 // 2].bitcast(BF16)

    o_cos = alloc(T)
    o_sin = alloc(T)
    o_identb = alloc(64)
    o_identf = alloc(128)
    o_ones = alloc(32)
    o_mur = alloc(NC)
    o_muk = alloc(NC)
    o_eps = alloc(16)
    o_stats = alloc(16)
    o_rs = alloc(16)
    o_rd = alloc(512)
    o_w = alloc(12288)
    o_wr, o_wk, o_wv = o_w, o_w + 4096, o_w + 8192
    YW = YPAD + T + 2
    o_y = alloc(NC * YW // 2)
    o_x = alloc(NT * D)
    o_strip = o_x
    o_qrot = o_x + 8192
    o_krot = o_x + 9216
    o_vg = o_x + 10240
    SCR = 11264
    o_s = alloc(SCR)
    o_xr = o_s
    o_xk = o_s + 2048
    o_kact = o_s + 4096
    o_krelu = o_s + 6144
    o_rsm = o_s + 7168
    o_prod = o_s + 8192
    o_dtmp = o_s + 10240
    o_ynat = o_s + 2048
    o_attnout = o_s
    o_rope1 = o_s + 8192
    o_rope2 = o_s + 8704
    o_lnst = o_s + 9216
    o_sqscr = o_s + 10240

    psF = nc.alloc_psum_tensor("psF", [128, 3072], F32).ap()
    psB = nc.alloc_psum_tensor("psB", [128, 2048], BF16).ap()

    def bank(i, w=512):
        return psF[:, i * 512:i * 512 + w]

    P = Plan()

    def dma(q, dst, src, wait=(), inc=None):
        P.op(q, lambda e, dst=dst, src=src: e.dma_start(out=dst, in_=src),
             wait=wait, inc=inc)

    COS = fview(o_cos, T)
    SIN = fview(o_sin, T)
    IDB = bview(o_identb, 128)
    IDF = fview(o_identf, 128)
    ONES = bview(o_ones, 64)
    MUR = fview(o_mur, NC)
    MUK = fview(o_muk, NC)
    EPS = fview(o_eps, 1)
    RD = fview(o_rd, 512)
    WQKV = bview(o_w, 24576).rearrange("p (c n) -> p c n", c=NC)
    WPROJ = bview(o_w, 8192).rearrange("p (c n) -> p c n", c=NC)
    WR = bview(o_wr, 8192).rearrange("p (c n) -> p c n", c=NC)
    WK = bview(o_wk, 8192).rearrange("p (c n) -> p c n", c=NC)
    WV = bview(o_wv, 8192).rearrange("p (c n) -> p c n", c=NC)
    Y = bview(o_y, NC * YW).rearrange("p (r n) -> p r n", r=NC)
    XN = fview(o_x, NT * D).rearrange("p (t d) -> p t d", t=NT)
    STRIP = bview(o_strip, 16384).rearrange("p (h k n) -> p h k n", h=2, k=NT)
    QROT = bview(o_qrot, 2048)
    KROT = bview(o_krot, 2048)
    VG = bview(o_vg, 4096).rearrange("p (t n) -> p t n", t=NT)
    XR = bview(o_xr, 4096).rearrange("p (c n) -> p c n", c=NC)
    XK = bview(o_xk, 4096).rearrange("p (c n) -> p c n", c=NC)
    KACT = bview(o_kact, 4096).rearrange("p (c n) -> p c n", c=NC)
    KRELU = fview(o_krelu, 1024).rearrange("p (a n) -> p a n", a=2)
    RSM = fview(o_rsm, 1024).rearrange("p (a n) -> p a n", a=2)
    PROD = fview(o_prod, 2048).rearrange("p (a n) -> p a n", a=4)
    DTMP = fview(o_dtmp, 512)
    YNAT = bview(o_ynat, 4096).rearrange("p (a n) -> p a n", a=4)
    ATTO = bview(o_attnout, 16384).rearrange("p (r n) -> p r n", r=NC)
    ROPE1 = fview(o_rope1, 512)
    ROPE2 = fview(o_rope2, 512)
    S0 = fview(o_lnst + 0, 16)
    S1 = fview(o_lnst + 16, 16)
    NEGMEAN = fview(o_lnst + 32, 16)
    EX2 = fview(o_lnst + 48, 16)
    MSQ = fview(o_lnst + 64, 16)
    VAR = fview(o_lnst + 80, 16)
    SD = fview(o_lnst + 96, 16)
    RSTD = fview(o_lnst + 112, 16)
    NMB = fview(o_lnst + 128, 16)
    SQSCR = fview(o_sqscr, 1024)

    uid = [0]

    def ev(tag):
        uid[0] += 1
        return f"{tag}.{uid[0]}"

    dma("sync", COS, cos_d, inc="c_cos")
    dma("sync", SIN, sin_d, inc="c_sin")
    dma("sync", IDB, identb_d, inc="c_idb")
    dma("sync", IDF, identf_d, inc="c_idf")
    dma("sync", ONES, ones_d, inc="c_ones")
    dma("sync", MUR, mur_d, inc="c_mur")
    dma("sync", MUK, muk_d, inc="c_muk")
    P.op("vector", lambda e: e.memset(EPS, 1e-5), inc="c_eps")
    P.op("vector", lambda e: e.memset(Y[:, :, 0:YPAD], 0.0), inc="c_y0")

    def load_ffn_w(tag, wait=()):
        evs = []
        for nm, off, src in (("r", o_wr, wrT_d), ("k", o_wk, wkT_d),
                             ("v", o_wv, wvT_d)):
            e_name = f"wf{nm}.{tag}"
            dma("gpsimd", bview(off, 8192).rearrange("p (c n) -> p c n", c=NC),
                src.rearrange("(c p) n -> p c n", p=128), wait=wait, inc=e_name)
            evs.append(e_name)
        return evs

    def load_wqkv(tag, wait=()):
        dma("gpsimd", bview(o_w, 24576).rearrange("p (c n) -> p c n", c=NC),
            wqkv_d.rearrange("(c p) n -> p c n", p=128), wait=wait,
            inc=f"wqkv.{tag}")
        return [f"wqkv.{tag}"]

    def load_wproj(tag, wait=()):
        dma("gpsimd", bview(o_w, 8192).rearrange("p (c n) -> p c n", c=NC),
            wproj_d.rearrange("(c p) n -> p c n", p=128), wait=wait,
            inc=f"wproj.{tag}")
        return [f"wproj.{tag}"]

    def load_x(b, tag, wait=()):
        dma("sync", XN,
            xin[b].rearrange("(t p) d -> p t d", p=128), wait=wait,
            inc=f"x.{tag}")
        return f"x.{tag}"

    def ln_phase(b, sub, x_deps):
        tag = f"ln{b}{sub}"
        rs_ev = sq_ev = None
        for t in range(NT):
            rs_ev = ev(f"{tag}.rs")
            P.op("vector", lambda e, t=t: e.reduce_sum(
                S0[:, t:t + 1], XN[:, t, :], axis=AX.X),
                wait=list(x_deps), inc=rs_ev)
            sq_ev = ev(f"{tag}.sq")
            P.op("scalar", lambda e, t=t: e.activation(
                out=SQSCR, in_=XN[:, t, :], func=AF.Square,
                accum_out=S1[:, t:t + 1]),
                wait=list(x_deps), inc=sq_ev)
        e_nm = ev(f"{tag}.nm")
        P.op("vector", lambda e: e.tensor_scalar(
            NEGMEAN, S0, -1.0 / D, None, ALU.mult), wait=[rs_ev], inc=e_nm)
        e_e2 = ev(f"{tag}.e2")
        P.op("vector", lambda e: e.tensor_scalar(
            EX2, S1, 1.0 / D, None, ALU.mult), wait=[sq_ev], inc=e_e2)
        e_ms = ev(f"{tag}.ms")
        P.op("vector", lambda e: e.tensor_tensor(
            MSQ, NEGMEAN, NEGMEAN, ALU.mult), wait=[e_nm], inc=e_ms)
        e_va = ev(f"{tag}.va")
        P.op("vector", lambda e: e.tensor_tensor(
            VAR, EX2, MSQ, ALU.subtract), wait=[e_e2, e_ms], inc=e_va)
        e_sd = ev(f"{tag}.sd")
        P.op("scalar", lambda e: e.activation(
            out=SD, in_=VAR, func=AF.Sqrt, bias=EPS),
            wait=[e_va, "c_eps"], inc=e_sd)
        e_rstd = ev(f"{tag}.rstd")
        P.op("vector", lambda e: e.reciprocal(RSTD, SD), wait=[e_sd],
             inc=e_rstd)
        e_nb = ev(f"{tag}.nb")
        P.op("vector", lambda e: e.tensor_tensor(
            NMB, NEGMEAN, RSTD, ALU.mult), wait=[e_rstd], inc=e_nb)
        act_ev = [None] * NT
        for t in range(NT):
            e_act = f"{tag}.act{t}"
            w = [e_nb, e_rstd]
            if t >= 4:
                w.append(f"{tag}.tp{t // 4 - 1}.{NC-1}")
            P.op("scalar", lambda e, t=t, slot=t % 4: e.activation(
                out=YNAT[:, slot, :], in_=XN[:, t, :], func=AF.Identity,
                bias=NMB[:, t:t + 1], scale=RSTD[:, t:t + 1]),
                wait=w, inc=e_act)
            act_ev[t] = e_act
            if t % 4 == 3:
                g = t // 4
                for r in range(NC):
                    e_tp = f"{tag}.tp{g}.{r}"
                    def mk_tp(r=r):
                        def f(e):
                            last = None
                            for j in range(4):
                                last = e.transpose(
                                    psB[:, (r % 2) * 1024 + j * 128:
                                        (r % 2) * 1024 + j * 128 + 128],
                                    YNAT[:, j, r * 128:(r + 1) * 128], IDB)
                            return last
                        return f
                    w = [act_ev[g * 4 + j] for j in range(4)] + ["c_idb"]
                    if g > 0 or r >= 2:
                        pg = g if r >= 2 else g - 1
                        pr = r - 2 if r >= 2 else r + NC - 2
                        w.append(f"{tag}.evc{pg}.{pr}")
                    P.op("tensor", mk_tp(), wait=w, inc=e_tp)
                    P.op("vector", lambda e, r=r, g=g: e.tensor_copy(
                        Y[:, r, YPAD + g * 512: YPAD + g * 512 + 512],
                        psB[:, (r % 2) * 1024:(r % 2) * 1024 + 512]),
                        wait=[e_tp, "c_y0"], inc=f"{tag}.evc{g}.{r}")
        return [f"{tag}.evc3.{r}" for r in range(NC)]

    def ffn_phase(b, sub, w_evs, y_done, x_deps):
        tag = f"f{b}{sub}"
        prev_chunk = []
        adds = []
        for c in range(NQC):
            ybase = YPAD + c * QC
            mix_ev = []
            for r in range(NC):
                y_c = Y[:, r, ybase:ybase + QC]
                y_p = Y[:, r, ybase - 1:ybase - 1 + QC]
                e_d = ev(f"{tag}.d")
                P.op("vector", lambda e, y_c=y_c, y_p=y_p: e.tensor_tensor(
                    DTMP, y_c, y_p, ALU.subtract),
                    wait=list(y_done) + prev_chunk, inc=e_d)
                e_xr = ev(f"{tag}.xr")
                P.op("vector", lambda e, r=r, y_p=y_p: e.scalar_tensor_tensor(
                    XR[:, r, :], DTMP, MUR[:, r:r + 1], y_p, ALU.mult, ALU.add),
                    wait=["c_mur", e_d], inc=e_xr)
                e_xk = ev(f"{tag}.xk")
                P.op("vector", lambda e, r=r, y_p=y_p: e.scalar_tensor_tensor(
                    XK[:, r, :], DTMP, MUK[:, r:r + 1], y_p, ALU.mult, ALU.add),
                    wait=["c_muk", e_d], inc=e_xk)
                mix_ev += [e_xr, e_xk]
            kact_ev = []
            for mt in range(NC):
                bk = mt % 2
                e_mm = ev(f"{tag}.kmm")
                def mk_k(mt=mt, bk=bk):
                    def f(e):
                        last = None
                        for cc in range(NC):
                            last = e.matmul(bank(bk),
                                            WK[:, cc, mt * 128:(mt + 1) * 128],
                                            XK[:, cc, :], start=(cc == 0),
                                            stop=(cc == NC - 1))
                        return last
                    return f
                w = mix_ev + list(w_evs)
                if mt >= 2:
                    w.append(f"{tag}.krl{c}.{mt-2}")
                P.op("tensor", mk_k(), wait=w, inc=e_mm)
                e_rl = f"{tag}.krl{c}.{mt}"
                P.op("scalar", lambda e, bk=bk: e.activation(
                    out=KRELU[:, bk, :], in_=bank(bk), func=AF.Relu),
                    wait=[e_mm], inc=e_rl)
                e_sq = ev(f"{tag}.ksq")
                P.op("scalar", lambda e, mt=mt, bk=bk: e.activation(
                    out=KACT[:, mt, :], in_=KRELU[:, bk, :], func=AF.Square),
                    wait=[e_rl], inc=e_sq)
                kact_ev.append(e_sq)
            for half in range(2):
                prod_ev = []
                for i in range(4):
                    mt = half * 4 + i
                    bkr = 2 + (i % 2)
                    bkv = 4 + (i % 2)
                    e_r = ev(f"{tag}.rmm")
                    def mk_r(mt=mt, bkr=bkr):
                        def f(e):
                            last = None
                            for cc in range(NC):
                                last = e.matmul(bank(bkr),
                                                WR[:, cc, mt * 128:(mt + 1) * 128],
                                                XR[:, cc, :], start=(cc == 0),
                                                stop=(cc == NC - 1))
                            return last
                        return f
                    w = mix_ev + list(w_evs)
                    if mt >= 2:
                        w.append(f"{tag}.sg{c}.{mt-2}")
                    P.op("tensor", mk_r(), wait=w, inc=e_r)
                    e_sg = f"{tag}.sg{c}.{mt}"
                    w = [e_r]
                    if mt >= 2:
                        w.append(f"{tag}.pr{c}.{mt-2}")
                    P.op("scalar", lambda e, i=i, bkr=bkr: e.activation(
                        out=RSM[:, i % 2, :], in_=bank(bkr), func=AF.Sigmoid),
                        wait=w, inc=e_sg)
                    e_v = ev(f"{tag}.vmm")
                    def mk_v(mt=mt, bkv=bkv):
                        def f(e):
                            last = None
                            for cc in range(NC):
                                last = e.matmul(bank(bkv),
                                                WV[:, cc, mt * 128:(mt + 1) * 128],
                                                KACT[:, cc, :], start=(cc == 0),
                                                stop=(cc == NC - 1))
                            return last
                        return f
                    w = kact_ev + list(w_evs)
                    if mt >= 2:
                        w.append(f"{tag}.pr{c}.{mt-2}")
                    P.op("tensor", mk_v(), wait=w, inc=e_v)
                    e_pr = f"{tag}.pr{c}.{mt}"
                    P.op("vector", lambda e, i=i, bkv=bkv: e.tensor_tensor(
                        PROD[:, i, :], RSM[:, i % 2, :], bank(bkv), ALU.mult),
                        wait=[e_sg, e_v], inc=e_pr)
                    prod_ev.append(e_pr)
                for tb in range(4):
                    bkT = tb % 2
                    e_tp = f"{tag}.tp{c}.{half}.{tb}"
                    def mk_tp(tb=tb, bkT=bkT):
                        def f(e):
                            last = None
                            for i in range(4):
                                last = e.transpose(
                                    psF[:, bkT * 512 + i * 128:
                                        bkT * 512 + i * 128 + 128],
                                    PROD[:, i, tb * 128:(tb + 1) * 128], IDF)
                            return last
                        return f
                    w = prod_ev + ["c_idf"]
                    if tb >= 2:
                        w.append(f"{tag}.ad{c}.{half}.{tb-2}")
                    elif half == 1:
                        w.append(f"{tag}.ad{c}.0.{tb+2}")
                    P.op("tensor", mk_tp(), wait=w, inc=e_tp)
                    e_ad = f"{tag}.ad{c}.{half}.{tb}"
                    t_idx = c * 4 + tb
                    P.op("vector", lambda e, t_idx=t_idx, half=half, bkT=bkT:
                         e.scalar_tensor_tensor(
                             XN[:, t_idx, half * 512:half * 512 + 512],
                             bank(bkT), 0.5,
                             XN[:, t_idx, half * 512:half * 512 + 512],
                             ALU.mult, ALU.add),
                         wait=[e_tp] + list(x_deps), inc=e_ad)
                    adds.append(e_ad)
            prev_chunk = [adds[-1]]
        return adds

    def attn_phase(b, y_done, wq_evs, spill_done):
        tag = f"a{b}"
        no_evs = []
        prev_group_pv = []
        last_exp = {0: None, 1: None}
        last_no = [None]
        last_rc = [None]
        gate = list(y_done) + [spill_done]
        for g in range(4):
            vev = []
            for nt in range(NT):
                bk = nt % 2
                e_v = ev(f"{tag}.vmm")
                def mk_v(nt=nt, g=g, bk=bk):
                    def f(e):
                        last = None
                        for cc in range(NC):
                            last = e.matmul(
                                bank(bk, 256),
                                Y[:, cc, YPAD + nt * 128:YPAD + nt * 128 + 128],
                                WQKV[:, cc, 2048 + g * 256:2048 + (g + 1) * 256],
                                start=(cc == 0), stop=(cc == NC - 1))
                        return last
                    return f
                w = gate + list(wq_evs)
                if nt >= 2:
                    w.append(f"{tag}.ve{g}.{nt-2}")
                elif g > 0:
                    w.append(f"{tag}.ve{g-1}.{NT-2+nt}")
                    w += prev_group_pv
                P.op("tensor", mk_v(), wait=w, inc=e_v)
                P.op("vector", lambda e, nt=nt, bk=bk: e.tensor_copy(
                    VG[:, nt, :], bank(bk, 256)), wait=[e_v],
                    inc=f"{tag}.ve{g}.{nt}")
                vev.append(f"{tag}.ve{g}.{nt}")
            group_pv = []
            for ph in range(2):
                p = g * 2 + ph
                rope_done = []
                for which, dst, colbase in (("q", QROT, p * 128),
                                            ("k", KROT, 1024 + p * 128)):
                    for ncq in range(NQC):
                        bk = (2 if which == "q" else 4) + (ncq % 2)
                        e_mm = ev(f"{tag}.qk")
                        def mk_qk(ncq=ncq, colbase=colbase, bk=bk):
                            def f(e):
                                last = None
                                for cc in range(NC):
                                    last = e.matmul(
                                        bank(bk),
                                        WQKV[:, cc, colbase:colbase + 128],
                                        Y[:, cc, YPAD + ncq * QC:
                                          YPAD + ncq * QC + QC],
                                        start=(cc == 0), stop=(cc == NC - 1))
                                return last
                            return f
                        w = gate + list(wq_evs)
                        if ncq >= 2:
                            w.append(f"{tag}.rp{p}{which}{ncq-2}")
                        elif which == "q":
                            w.append(last_exp[1])
                        else:
                            w.append(last_no[0])
                            w.append(last_rc[0])
                        P.op("tensor", mk_qk(), wait=w, inc=e_mm)
                        cs = COS[:, ncq * QC:(ncq + 1) * QC]
                        sn = SIN[:, ncq * QC:(ncq + 1) * QC]
                        pb = bank(bk)
                        e_r1 = ev(f"{tag}.r1")
                        P.op("vector", lambda e, pb=pb, cs=cs: e.tensor_tensor(
                            ROPE1, pb, cs, ALU.mult),
                            wait=[e_mm, "c_cos"], inc=e_r1)
                        e_r2 = ev(f"{tag}.r2")
                        def mk_r2(pb=pb, sn=sn):
                            def f(e):
                                last = None
                                for blk in range(4):
                                    ob = (blk // 2) * 64 + (blk % 2) * 32
                                    sb = (blk // 2) * 64 + (1 - blk % 2) * 32
                                    last = e.tensor_tensor(
                                        ROPE2[ob:ob + 32, :],
                                        pb[sb:sb + 32, :],
                                        sn[ob:ob + 32, :], ALU.mult)
                                return last
                            return f
                        P.op("vector", mk_r2(), wait=[e_mm, "c_sin"], inc=e_r2)
                        e_rp = f"{tag}.rp{p}{which}{ncq}"
                        P.op("vector", lambda e, dst=dst, ncq=ncq: e.tensor_tensor(
                            dst[:, ncq * QC:(ncq + 1) * QC], ROPE1, ROPE2,
                            ALU.add), wait=[e_r1, e_r2], inc=e_rp)
                        rope_done.append(e_rp)
                for qc in range(NQC):
                    exp_evs = []
                    for kb2 in range(8):
                        for h in range(2):
                            e_s = ev(f"{tag}.sc")
                            def mk_s(h=h, kb2=kb2, qc=qc):
                                def f(e):
                                    last = None
                                    for j in range(2):
                                        kb = kb2 * 2 + j
                                        last = e.matmul(
                                            bank(2 * h + j),
                                            KROT[h * 64:(h + 1) * 64,
                                                 kb * 128:(kb + 1) * 128],
                                            QROT[h * 64:(h + 1) * 64,
                                                 qc * QC:(qc + 1) * QC],
                                            start=True, stop=True,
                                            tile_position=(h * 64, 0))
                                    return last
                                return f
                            w = list(rope_done) + [last_exp[h]]
                            P.op("tensor", mk_s(), wait=w, inc=e_s)
                            e_x = f"{tag}.ex{p}.{qc}.{h}.{kb2}"
                            P.op("scalar", lambda e, h=h, kb2=kb2:
                                 e.activation(
                                     out=STRIP[:, h, kb2 * 2:kb2 * 2 + 2, :],
                                     in_=psF[:, h * 1024:h * 1024 + 1024],
                                     func=AF.Exp, scale=0.125),
                                 wait=[e_s], inc=e_x)
                            last_exp[h] = e_x
                            exp_evs.append(e_x)
                    e_pv1 = ev(f"{tag}.pv1")
                    def mk_pv1(ph=ph, qc=qc):
                        def f(e):
                            last = None
                            for kt in range(NT):
                                last = e.matmul(
                                    bank(4)[0:64, :],
                                    VG[:, kt, (ph * 2) * 64:
                                       (ph * 2 + 1) * 64],
                                    STRIP[:, 0, kt, :],
                                    start=(kt == 0), stop=(kt == NT - 1),
                                    tile_position=(0, 0))
                                last = e.matmul(
                                    bank(5)[64:128, :],
                                    ONES, STRIP[:, 1, kt, :],
                                    start=(kt == 0), stop=(kt == NT - 1),
                                    tile_position=(0, 64))
                            return last
                        return f
                    w = exp_evs + vev + ["c_ones"]
                    w.append(last_no[0])
                    w.append(last_rc[0])
                    P.op("tensor", mk_pv1(), wait=w, inc=e_pv1)
                    e_pv2 = ev(f"{tag}.pv2")
                    def mk_pv2(ph=ph, qc=qc):
                        def f(e):
                            last = None
                            for kt in range(NT):
                                last = e.matmul(
                                    bank(5)[0:64, :],
                                    ONES, STRIP[:, 0, kt, :],
                                    start=(kt == 0), stop=(kt == NT - 1),
                                    tile_position=(0, 0))
                                last = e.matmul(
                                    bank(4)[64:128, :],
                                    VG[:, kt, (ph * 2 + 1) * 64:
                                       (ph * 2 + 2) * 64],
                                    STRIP[:, 1, kt, :],
                                    start=(kt == 0), stop=(kt == NT - 1),
                                    tile_position=(0, 64))
                            return last
                        return f
                    P.op("tensor", mk_pv2(),
                         wait=exp_evs + vev + ["c_ones", e_pv1], inc=e_pv2)
                    e_rc = f"{tag}.rc{p}.{qc}"
                    P.op("vector", lambda e: e.reciprocal(RD, bank(5)),
                         wait=[e_pv1, e_pv2], inc=e_rc)
                    last_rc[0] = e_rc
                    e_no = f"{tag}.no{p}.{qc}"
                    P.op("vector", lambda e, p=p, qc=qc: e.tensor_tensor(
                        ATTO[:, p, qc * QC:(qc + 1) * QC], bank(4), RD,
                        ALU.mult), wait=[e_rc, e_pv2], inc=e_no)
                    last_no[0] = e_no
                    no_evs.append(e_no)
                group_pv.append(f"{tag}.no{p}.{NQC-1}")
            prev_group_pv = group_pv
        wproj_ev = load_wproj(tag, wait=[no_evs[-1]])
        xrel = []
        for t in range(NT):
            dma("sync", XN[:, t, :],
                xspill[b].rearrange("(t p) d -> p t d", p=128)[:, t, :],
                wait=[no_evs[-1]] if t == 0 else [], inc=f"{tag}.xr{t}")
            xrel.append(f"{tag}.xr{t}")
        adds = []
        for qc in range(NQC):
            for mt in range(NC):
                bk = mt % 2
                e_mm = ev(f"{tag}.pj")
                def mk_pj(mt=mt, qc=qc, bk=bk):
                    def f(e):
                        last = None
                        for cc in range(NC):
                            last = e.matmul(
                                bank(bk), WPROJ[:, cc, mt * 128:(mt + 1) * 128],
                                ATTO[:, cc, qc * QC:(qc + 1) * QC],
                                start=(cc == 0), stop=(cc == NC - 1))
                        return last
                    return f
                w = list(wproj_ev) + [no_evs[-1]]
                if mt >= 2:
                    w.append(f"{tag}.pc{qc}.{mt-2}")
                elif qc > 0:
                    w.append(f"{tag}.pc{qc-1}.{mt+6}")
                P.op("tensor", mk_pj(), wait=w, inc=e_mm)
                P.op("vector", lambda e, slot=mt % 4, bk=bk: e.tensor_copy(
                    PROD[:, slot, :], bank(bk)), wait=[e_mm],
                    inc=f"{tag}.pc{qc}.{mt}")
                if mt % 4 == 3:
                    half = mt // 4
                    for tb in range(4):
                        bkT = 2 + (tb % 2)
                        e_tp = f"{tag}.ptp{qc}.{half}.{tb}"
                        def mk_tp(tb=tb, bkT=bkT):
                            def f(e):
                                last = None
                                for i in range(4):
                                    last = e.transpose(
                                        psF[:, bkT * 512 + i * 128:
                                            bkT * 512 + i * 128 + 128],
                                        PROD[:, i, tb * 128:(tb + 1) * 128], IDF)
                                return last
                            return f
                        w = [f"{tag}.pc{qc}.{half*4+j}" for j in range(4)]
                        w.append("c_idf")
                        if tb >= 2:
                            w.append(f"{tag}.pa{qc}.{half}.{tb-2}")
                        elif half == 1:
                            w.append(f"{tag}.pa{qc}.0.{tb+2}")
                        elif qc > 0:
                            w.append(f"{tag}.pa{qc-1}.1.{tb+2}")
                        P.op("tensor", mk_tp(), wait=w, inc=e_tp)
                        e_ad = f"{tag}.pa{qc}.{half}.{tb}"
                        t_idx = qc * 4 + tb
                        P.op("vector", lambda e, t_idx=t_idx, half=half,
                             bkT=bkT: e.tensor_tensor(
                                 XN[:, t_idx, half * 512:half * 512 + 512],
                                 bank(bkT),
                                 XN[:, t_idx, half * 512:half * 512 + 512],
                                 ALU.add), wait=[e_tp, xrel[t_idx]], inc=e_ad)
                        adds.append(e_ad)
        return adds

    ffn_w_ev = load_ffn_w("init")
    out_evs = []
    prev_out = None
    wf = ffn_w_ev
    for b in range(B_PER):
        xev = load_x(b, f"b{b}", wait=[prev_out] if prev_out else [])
        y1 = ln_phase(b, 1, [xev])
        f1 = ffn_phase(b, 1, wf, y1, [xev])
        spill_evs = []
        for c in range(NQC):
            dma("sync",
                xspill[b].rearrange("(g p) d -> p g d", p=128)[:, c * 4:c * 4 + 4, :],
                XN[:, c * 4:c * 4 + 4, :],
                wait=[f1[c * 8 + 7]], inc=f"spill{b}.{c}")
            spill_evs.append(f"spill{b}.{c}")
        y2 = ln_phase(b, 2, [f1[-1]])
        wq = load_wqkv(f"b{b}", wait=[f1[-1]])
        a2 = attn_phase(b, y2, wq, spill_evs[-1])
        wf = load_ffn_w(f"s3b{b}", wait=[a2[-1]])
        y3 = ln_phase(b, 3, [a2[-1]])
        f3 = ffn_phase(b, 3, wf, y3, a2[-8:])
        for c in range(NQC):
            dma("sync",
                out_d[b].rearrange("(g p) d -> p g d", p=128)[:, c * 4:c * 4 + 4, :],
                XN[:, c * 4:c * 4 + 4, :],
                wait=[f3[c * 8 + 7]], inc=f"out{b}.{c}")
        out_evs.append(f"out{b}.{NQC-1}")
        prev_out = f"out{b}.{NQC-1}"

    P.op("sync", lambda e: e.nop(), wait=out_evs)

    from contextlib import ExitStack
    with ExitStack() as ctx:
        e = ctx.enter_context
        sems = {name: e(nc.semaphore(f"sem_{name}")) for name in ENGINES}
        blk = e(nc.Block())

        @blk.tensor
        def _(te):
            P.emit("tensor", te, sems)

        @blk.vector
        def _(ve):
            P.emit("vector", ve, sems)

        @blk.scalar
        def _(sc):
            P.emit("scalar", sc, sems)

        @blk.sync
        def _(sy):
            P.emit("sync", sy, sems)

        @blk.gpsimd
        def _(gp):
            P.emit("gpsimd", gp, sems)

    return nc


_NC_CACHE = None


def kernel(L0_attn_x, L0_attn_w_qkv, L0_attn_w_proj, L0_ffn_W_r, L0_ffn_W_k,
           L0_ffn_W_v, L0_ffn_mu_r, L0_ffn_mu_k):
    global _NC_CACHE
    x = np.ascontiguousarray(np.asarray(L0_attn_x, dtype=np.float32))
    wqkv = np.asarray(L0_attn_w_qkv, dtype=np.float32)
    wproj = np.ascontiguousarray(np.asarray(L0_attn_w_proj, dtype=np.float32))
    wrT = np.ascontiguousarray(np.asarray(L0_ffn_W_r, dtype=np.float32).T)
    wkT = np.ascontiguousarray(np.asarray(L0_ffn_W_k, dtype=np.float32).T)
    wvT = np.ascontiguousarray(np.asarray(L0_ffn_W_v, dtype=np.float32).T)
    mur = np.ascontiguousarray(
        np.asarray(L0_ffn_mu_r, np.float32).reshape(NC, 128).T)
    muk = np.ascontiguousarray(
        np.asarray(L0_ffn_mu_k, np.float32).reshape(NC, 128).T)

    src = np.empty(HD, np.int64)
    src[0:32] = np.arange(0, 64, 2)
    src[32:64] = np.arange(1, 64, 2)
    colperm = np.concatenate([h * HD + src for h in range(H)])
    wqkv_p = np.ascontiguousarray(np.concatenate(
        [wqkv[:, 0:D][:, colperm], wqkv[:, D:2 * D][:, colperm],
         wqkv[:, 2 * D:3 * D]], axis=1))

    inv = 1.0 / 10000.0 ** (np.arange(0, HD, 2, dtype=np.float64) / HD)
    ang = np.arange(T, dtype=np.float64)[None, :] * inv[:, None]
    cos32 = np.cos(ang).astype(np.float32)
    sin32 = np.sin(ang).astype(np.float32)
    costab = np.ascontiguousarray(np.tile(np.concatenate([cos32, cos32], 0),
                                          (2, 1)))
    sintab = np.ascontiguousarray(np.tile(np.concatenate([-sin32, sin32], 0),
                                          (2, 1)))
    identb = np.eye(128, dtype=ml_dtypes.bfloat16)
    identf = np.eye(128, dtype=np.float32)
    onesb = np.ones((128, 64), dtype=ml_dtypes.bfloat16)

    if _NC_CACHE is None:
        _NC_CACHE = build_program()
    nc = _NC_CACHE

    in_maps = []
    for c in range(NCORES):
        in_maps.append({
            "x": np.ascontiguousarray(x[c * B_PER:(c + 1) * B_PER]),
            "wqkv": wqkv_p, "wproj": wproj,
            "wrT": wrT, "wkT": wkT, "wvT": wvT,
            "mur": mur, "muk": muk,
            "costab": costab, "sintab": sintab,
            "identb": identb, "identf": identf, "onesb": onesb,
        })
    res = run_bass_kernel_spmd(nc, in_maps, core_ids=list(range(NCORES)))
    outs = [np.asarray(res.results[c]["out"], dtype=np.float32)
            for c in range(NCORES)]
    return np.concatenate(outs, axis=0)


# version 06 (final)
# speedup vs baseline: 1.2238x; 1.1094x over previous
import numpy as np
import ml_dtypes
import concourse.bass as bass
import concourse.mybir as mybir
from concourse.bass_utils import run_bass_kernel_spmd

F32 = mybir.dt.float32
BF16 = mybir.dt.bfloat16
AF = mybir.ActivationFunctionType
ALU = mybir.AluOpType
AX = mybir.AxisListType

NCORES = 8
B_PER = 2
T = 2048
D = 1024
H = 16
HD = 64
NT = T // 128
NC = D // 128
NQC = 4
QC = 512
YPAD = 2

ENGINES = ("tensor", "vector", "scalar", "sync", "gpsimd")


class Plan:

    def __init__(self):
        self.steps = []
        self.events = {}
        self.counts = {e: 0 for e in ENGINES}

    def op(self, eng, fn, wait=(), inc=None):
        if inc is not None:
            amt = 16 if eng in ("sync", "gpsimd") else 1
            self.counts[eng] += amt
            assert inc not in self.events, f"duplicate event {inc}"
            self.events[inc] = (eng, self.counts[eng])
        self.steps.append((eng, fn, tuple(w for w in wait if w), inc))

    def emit(self, eng_name, eng_obj, sems):
        last = {}
        passed = 0
        for eng, fn, waits, inc in self.steps:
            if eng != eng_name:
                continue
            for w in waits:
                weng, val = self.events[w]
                if weng == eng_name:
                    assert val <= passed, f"forward same-engine wait {w}"
                if last.get(weng, 0) < val:
                    eng_obj.wait_ge(sems[weng], val)
                    last[weng] = val
            instr = fn(eng_obj)
            if inc is not None:
                amt = 16 if eng in ("sync", "gpsimd") else 1
                instr.then_inc(sems[eng], amt)
                passed += amt


def build_program():
    nc = bass.Bass("TRN2", target_bir_lowering=False)

    xin = nc.dram_tensor("x", [B_PER, T, D], F32, kind="ExternalInput").ap()
    wqkv_d = nc.dram_tensor("wqkv", [D, 3 * D], F32, kind="ExternalInput").ap()
    wproj_d = nc.dram_tensor("wproj", [D, D], F32, kind="ExternalInput").ap()
    wrT_d = nc.dram_tensor("wrT", [D, D], F32, kind="ExternalInput").ap()
    wkT_d = nc.dram_tensor("wkT", [D, D], F32, kind="ExternalInput").ap()
    wvT_d = nc.dram_tensor("wvT", [D, D], F32, kind="ExternalInput").ap()
    mur_d = nc.dram_tensor("mur", [128, NC], F32, kind="ExternalInput").ap()
    muk_d = nc.dram_tensor("muk", [128, NC], F32, kind="ExternalInput").ap()
    cos_d = nc.dram_tensor("costab", [128, T], F32, kind="ExternalInput").ap()
    sin_d = nc.dram_tensor("sintab", [128, T], F32, kind="ExternalInput").ap()
    identb_d = nc.dram_tensor("identb", [128, 128], BF16, kind="ExternalInput").ap()
    identf_d = nc.dram_tensor("identf", [128, 128], F32, kind="ExternalInput").ap()
    ones_d = nc.dram_tensor("onesb", [128, 64], BF16, kind="ExternalInput").ap()
    out_d = nc.dram_tensor("out", [B_PER, T, D], F32, kind="ExternalOutput").ap()
    xspill = nc.dram_tensor("xspill", [B_PER, T, D], F32, kind="Internal").ap()

    AW = 53100
    arena = nc.alloc_sbuf_tensor("arena", [128, AW], F32)
    arena_ap = arena.ap()
    cur = [0]

    def alloc(words, align=16):
        cur[0] = (cur[0] + align - 1) // align * align
        off = cur[0]
        cur[0] += words
        assert cur[0] <= AW, f"arena overflow: {cur[0]} > {AW}"
        return off

    def fview(off, words):
        return arena_ap[:, off:off + words]

    def bview(off, n_bf16):
        return arena_ap[:, off:off + n_bf16 // 2].bitcast(BF16)

    o_cos = alloc(T)
    o_sin = alloc(T)
    o_identb = alloc(64)
    o_identf = alloc(128)
    o_ones = alloc(32)
    o_mur = alloc(NC)
    o_muk = alloc(NC)
    o_eps = alloc(16)
    o_stats = alloc(16)
    o_rs = alloc(16)
    o_rd = alloc(512)
    o_w = alloc(12288)
    o_wr, o_wk, o_wv = o_w, o_w + 4096, o_w + 8192
    YW = YPAD + T + 2
    o_y = alloc(NC * YW // 2)
    o_x = alloc(NT * D)
    o_strip = o_x
    o_qrot = o_x + 8192
    o_krot = o_x + 9216
    o_vg = o_x + 10240
    SCR = 11264
    o_s = alloc(SCR)
    o_xr = o_s
    o_xk = o_s + 2048
    o_kact = o_s + 4096
    o_krelu = o_s + 6144
    o_rsm = o_s + 7168
    o_prod = o_s + 8192
    o_dtmp = o_s + 10240
    o_ynat = o_s + 2048
    o_attnout = o_s
    o_rope1 = o_s + 8192
    o_rope2 = o_s + 8704
    o_lnst = o_s + 9216
    o_sqscr = o_s + 10240

    psF = nc.alloc_psum_tensor("psF", [128, 3072], F32).ap()
    psB = nc.alloc_psum_tensor("psB", [128, 2048], BF16).ap()

    def bank(i, w=512):
        return psF[:, i * 512:i * 512 + w]

    P = Plan()

    def dma(q, dst, src, wait=(), inc=None):
        P.op(q, lambda e, dst=dst, src=src: e.dma_start(out=dst, in_=src),
             wait=wait, inc=inc)

    COS = fview(o_cos, T)
    SIN = fview(o_sin, T)
    IDB = bview(o_identb, 128)
    IDF = fview(o_identf, 128)
    ONES = bview(o_ones, 64)
    MUR = fview(o_mur, NC)
    MUK = fview(o_muk, NC)
    EPS = fview(o_eps, 1)
    RD = fview(o_rd, 512)
    WQKV = bview(o_w, 24576).rearrange("p (c n) -> p c n", c=NC)
    WPROJ = bview(o_w, 8192).rearrange("p (c n) -> p c n", c=NC)
    WR = bview(o_wr, 8192).rearrange("p (c n) -> p c n", c=NC)
    WK = bview(o_wk, 8192).rearrange("p (c n) -> p c n", c=NC)
    WV = bview(o_wv, 8192).rearrange("p (c n) -> p c n", c=NC)
    Y = bview(o_y, NC * YW).rearrange("p (r n) -> p r n", r=NC)
    XN = fview(o_x, NT * D).rearrange("p (t d) -> p t d", t=NT)
    STRIP = bview(o_strip, 16384).rearrange("p (h k n) -> p h k n", h=2, k=NT)
    QROT = bview(o_qrot, 2048)
    KROT = bview(o_krot, 2048)
    VG = bview(o_vg, 4096).rearrange("p (t n) -> p t n", t=NT)
    XR = bview(o_xr, 4096).rearrange("p (c n) -> p c n", c=NC)
    XK = bview(o_xk, 4096).rearrange("p (c n) -> p c n", c=NC)
    KACT = bview(o_kact, 4096).rearrange("p (c n) -> p c n", c=NC)
    KRELU = fview(o_krelu, 1024).rearrange("p (a n) -> p a n", a=2)
    RSM = fview(o_rsm, 1024).rearrange("p (a n) -> p a n", a=2)
    PROD = fview(o_prod, 2048).rearrange("p (a n) -> p a n", a=4)
    DTMP = fview(o_dtmp, 512)
    YNAT = bview(o_ynat, 4096).rearrange("p (a n) -> p a n", a=4)
    ATTO = bview(o_attnout, 16384).rearrange("p (r n) -> p r n", r=NC)
    ROPE1 = fview(o_rope1, 512)
    ROPE2 = fview(o_rope2, 512)
    S0 = fview(o_lnst + 0, 16)
    S1 = fview(o_lnst + 16, 16)
    NEGMEAN = fview(o_lnst + 32, 16)
    EX2 = fview(o_lnst + 48, 16)
    MSQ = fview(o_lnst + 64, 16)
    VAR = fview(o_lnst + 80, 16)
    SD = fview(o_lnst + 96, 16)
    RSTD = fview(o_lnst + 112, 16)
    NMB = fview(o_lnst + 128, 16)
    SQSCR = fview(o_sqscr, 1024)

    uid = [0]

    def ev(tag):
        uid[0] += 1
        return f"{tag}.{uid[0]}"

    dma("sync", COS, cos_d, inc="c_cos")
    dma("sync", SIN, sin_d, inc="c_sin")
    dma("sync", IDB, identb_d, inc="c_idb")
    dma("sync", IDF, identf_d, inc="c_idf")
    dma("sync", ONES, ones_d, inc="c_ones")
    dma("sync", MUR, mur_d, inc="c_mur")
    dma("sync", MUK, muk_d, inc="c_muk")
    P.op("vector", lambda e: e.memset(EPS, 1e-5), inc="c_eps")
    P.op("vector", lambda e: e.memset(Y[:, :, 0:YPAD], 0.0), inc="c_y0")

    def load_ffn_w(tag, wait=()):
        evs = []
        for nm, off, src in (("r", o_wr, wrT_d), ("k", o_wk, wkT_d),
                             ("v", o_wv, wvT_d)):
            e_name = f"wf{nm}.{tag}"
            dma("gpsimd", bview(off, 8192).rearrange("p (c n) -> p c n", c=NC),
                src.rearrange("(c p) n -> p c n", p=128), wait=wait, inc=e_name)
            evs.append(e_name)
        return evs

    def load_wqkv(tag, wait=()):
        dma("gpsimd", bview(o_w, 24576).rearrange("p (c n) -> p c n", c=NC),
            wqkv_d.rearrange("(c p) n -> p c n", p=128), wait=wait,
            inc=f"wqkv.{tag}")
        return [f"wqkv.{tag}"]

    def load_wproj(tag, wait=()):
        dma("gpsimd", bview(o_w, 8192).rearrange("p (c n) -> p c n", c=NC),
            wproj_d.rearrange("(c p) n -> p c n", p=128), wait=wait,
            inc=f"wproj.{tag}")
        return [f"wproj.{tag}"]

    def load_x(b, tag, wait=()):
        dma("sync", XN,
            xin[b].rearrange("(t p) d -> p t d", p=128), wait=wait,
            inc=f"x.{tag}")
        return f"x.{tag}"

    def ln_phase(b, sub, x_deps):
        tag = f"ln{b}{sub}"
        rs_ev = sq_ev = None
        for t in range(NT):
            rs_ev = ev(f"{tag}.rs")
            P.op("vector", lambda e, t=t: e.reduce_sum(
                S0[:, t:t + 1], XN[:, t, :], axis=AX.X),
                wait=list(x_deps), inc=rs_ev)
            sq_ev = ev(f"{tag}.sq")
            P.op("scalar", lambda e, t=t: e.activation(
                out=SQSCR, in_=XN[:, t, :], func=AF.Square,
                accum_out=S1[:, t:t + 1]),
                wait=list(x_deps), inc=sq_ev)
        e_nm = ev(f"{tag}.nm")
        P.op("vector", lambda e: e.tensor_scalar(
            NEGMEAN, S0, -1.0 / D, None, ALU.mult), wait=[rs_ev], inc=e_nm)
        e_e2 = ev(f"{tag}.e2")
        P.op("vector", lambda e: e.tensor_scalar(
            EX2, S1, 1.0 / D, None, ALU.mult), wait=[sq_ev], inc=e_e2)
        e_ms = ev(f"{tag}.ms")
        P.op("vector", lambda e: e.tensor_tensor(
            MSQ, NEGMEAN, NEGMEAN, ALU.mult), wait=[e_nm], inc=e_ms)
        e_va = ev(f"{tag}.va")
        P.op("vector", lambda e: e.tensor_tensor(
            VAR, EX2, MSQ, ALU.subtract), wait=[e_e2, e_ms], inc=e_va)
        e_sd = ev(f"{tag}.sd")
        P.op("scalar", lambda e: e.activation(
            out=SD, in_=VAR, func=AF.Sqrt, bias=EPS),
            wait=[e_va, "c_eps"], inc=e_sd)
        e_rstd = ev(f"{tag}.rstd")
        P.op("vector", lambda e: e.reciprocal(RSTD, SD), wait=[e_sd],
             inc=e_rstd)
        e_nb = ev(f"{tag}.nb")
        P.op("vector", lambda e: e.tensor_tensor(
            NMB, NEGMEAN, RSTD, ALU.mult), wait=[e_rstd], inc=e_nb)
        act_ev = [None] * NT
        for t in range(NT):
            e_act = f"{tag}.act{t}"
            w = [e_nb, e_rstd]
            if t >= 4:
                w.append(f"{tag}.tp{t // 4 - 1}.{NC-1}")
            P.op("scalar", lambda e, t=t, slot=t % 4: e.activation(
                out=YNAT[:, slot, :], in_=XN[:, t, :], func=AF.Identity,
                bias=NMB[:, t:t + 1], scale=RSTD[:, t:t + 1]),
                wait=w, inc=e_act)
            act_ev[t] = e_act
            if t % 4 == 3:
                g = t // 4
                for r in range(NC):
                    e_tp = f"{tag}.tp{g}.{r}"
                    def mk_tp(r=r):
                        def f(e):
                            last = None
                            for j in range(4):
                                last = e.transpose(
                                    psB[:, (r % 2) * 1024 + j * 128:
                                        (r % 2) * 1024 + j * 128 + 128],
                                    YNAT[:, j, r * 128:(r + 1) * 128], IDB)
                            return last
                        return f
                    w = [act_ev[g * 4 + j] for j in range(4)] + ["c_idb"]
                    if g > 0 or r >= 2:
                        pg = g if r >= 2 else g - 1
                        pr = r - 2 if r >= 2 else r + NC - 2
                        w.append(f"{tag}.evc{pg}.{pr}")
                    P.op("tensor", mk_tp(), wait=w, inc=e_tp)
                    P.op("vector", lambda e, r=r, g=g: e.tensor_copy(
                        Y[:, r, YPAD + g * 512: YPAD + g * 512 + 512],
                        psB[:, (r % 2) * 1024:(r % 2) * 1024 + 512]),
                        wait=[e_tp, "c_y0"], inc=f"{tag}.evc{g}.{r}")
        return [f"{tag}.evc3.{r}" for r in range(NC)]

    def ffn_phase(b, sub, w_evs, y_done, x_deps):
        tag = f"f{b}{sub}"
        prev_chunk = []
        adds = []
        for c in range(NQC):
            ybase = YPAD + c * QC
            mix_ev = []
            for r in range(NC):
                y_c = Y[:, r, ybase:ybase + QC]
                y_p = Y[:, r, ybase - 1:ybase - 1 + QC]
                e_d = ev(f"{tag}.d")
                P.op("vector", lambda e, y_c=y_c, y_p=y_p: e.tensor_tensor(
                    DTMP, y_c, y_p, ALU.subtract),
                    wait=list(y_done) + prev_chunk, inc=e_d)
                e_xr = ev(f"{tag}.xr")
                P.op("vector", lambda e, r=r, y_p=y_p: e.scalar_tensor_tensor(
                    XR[:, r, :], DTMP, MUR[:, r:r + 1], y_p, ALU.mult, ALU.add),
                    wait=["c_mur", e_d], inc=e_xr)
                e_xk = ev(f"{tag}.xk")
                P.op("vector", lambda e, r=r, y_p=y_p: e.scalar_tensor_tensor(
                    XK[:, r, :], DTMP, MUK[:, r:r + 1], y_p, ALU.mult, ALU.add),
                    wait=["c_muk", e_d], inc=e_xk)
                mix_ev += [e_xr, e_xk]
            kact_ev = []
            for mt in range(NC):
                bk = mt % 2
                e_mm = ev(f"{tag}.kmm")
                def mk_k(mt=mt, bk=bk):
                    def f(e):
                        last = None
                        for cc in range(NC):
                            last = e.matmul(bank(bk),
                                            WK[:, cc, mt * 128:(mt + 1) * 128],
                                            XK[:, cc, :], start=(cc == 0),
                                            stop=(cc == NC - 1))
                        return last
                    return f
                w = mix_ev + list(w_evs)
                if mt >= 2:
                    w.append(f"{tag}.krl{c}.{mt-2}")
                P.op("tensor", mk_k(), wait=w, inc=e_mm)
                e_rl = f"{tag}.krl{c}.{mt}"
                P.op("scalar", lambda e, bk=bk: e.activation(
                    out=KRELU[:, bk, :], in_=bank(bk), func=AF.Relu),
                    wait=[e_mm], inc=e_rl)
                e_sq = ev(f"{tag}.ksq")
                P.op("scalar", lambda e, mt=mt, bk=bk: e.activation(
                    out=KACT[:, mt, :], in_=KRELU[:, bk, :], func=AF.Square),
                    wait=[e_rl], inc=e_sq)
                kact_ev.append(e_sq)
            for half in range(2):
                prod_ev = []
                for i in range(4):
                    mt = half * 4 + i
                    bkr = 2 + (i % 2)
                    bkv = 4 + (i % 2)
                    e_r = ev(f"{tag}.rmm")
                    def mk_r(mt=mt, bkr=bkr):
                        def f(e):
                            last = None
                            for cc in range(NC):
                                last = e.matmul(bank(bkr),
                                                WR[:, cc, mt * 128:(mt + 1) * 128],
                                                XR[:, cc, :], start=(cc == 0),
                                                stop=(cc == NC - 1))
                            return last
                        return f
                    w = mix_ev + list(w_evs)
                    if mt >= 2:
                        w.append(f"{tag}.sg{c}.{mt-2}")
                    P.op("tensor", mk_r(), wait=w, inc=e_r)
                    e_sg = f"{tag}.sg{c}.{mt}"
                    w = [e_r]
                    if mt >= 2:
                        w.append(f"{tag}.pr{c}.{mt-2}")
                    P.op("scalar", lambda e, i=i, bkr=bkr: e.activation(
                        out=RSM[:, i % 2, :], in_=bank(bkr), func=AF.Sigmoid),
                        wait=w, inc=e_sg)
                    e_v = ev(f"{tag}.vmm")
                    def mk_v(mt=mt, bkv=bkv):
                        def f(e):
                            last = None
                            for cc in range(NC):
                                last = e.matmul(bank(bkv),
                                                WV[:, cc, mt * 128:(mt + 1) * 128],
                                                KACT[:, cc, :], start=(cc == 0),
                                                stop=(cc == NC - 1))
                            return last
                        return f
                    w = kact_ev + list(w_evs)
                    if mt >= 2:
                        w.append(f"{tag}.pr{c}.{mt-2}")
                    P.op("tensor", mk_v(), wait=w, inc=e_v)
                    e_pr = f"{tag}.pr{c}.{mt}"
                    P.op("vector", lambda e, i=i, bkv=bkv: e.tensor_tensor(
                        PROD[:, i, :], RSM[:, i % 2, :], bank(bkv), ALU.mult),
                        wait=[e_sg, e_v], inc=e_pr)
                    prod_ev.append(e_pr)
                for tb in range(4):
                    bkT = tb % 2
                    e_tp = f"{tag}.tp{c}.{half}.{tb}"
                    def mk_tp(tb=tb, bkT=bkT):
                        def f(e):
                            last = None
                            for i in range(4):
                                last = e.transpose(
                                    psF[:, bkT * 512 + i * 128:
                                        bkT * 512 + i * 128 + 128],
                                    PROD[:, i, tb * 128:(tb + 1) * 128], IDF)
                            return last
                        return f
                    w = prod_ev + ["c_idf"]
                    if tb >= 2:
                        w.append(f"{tag}.ad{c}.{half}.{tb-2}")
                    elif half == 1:
                        w.append(f"{tag}.ad{c}.0.{tb+2}")
                    P.op("tensor", mk_tp(), wait=w, inc=e_tp)
                    e_ad = f"{tag}.ad{c}.{half}.{tb}"
                    t_idx = c * 4 + tb
                    P.op("vector", lambda e, t_idx=t_idx, half=half, bkT=bkT:
                         e.scalar_tensor_tensor(
                             XN[:, t_idx, half * 512:half * 512 + 512],
                             bank(bkT), 0.5,
                             XN[:, t_idx, half * 512:half * 512 + 512],
                             ALU.mult, ALU.add),
                         wait=[e_tp] + list(x_deps), inc=e_ad)
                    adds.append(e_ad)
            prev_chunk = [adds[-1]]
        return adds

    def attn_phase(b, y_done, wq_evs, spill_done):
        tag = f"a{b}"
        no_evs = []
        prev_group_pv = []
        last_exp = {0: None, 1: None}
        last_no = [None]
        last_rc = [None]
        gate = list(y_done) + [spill_done]
        for g in range(4):
            vev = []
            for nt in range(NT):
                bk = nt % 2
                e_v = ev(f"{tag}.vmm")
                def mk_v(nt=nt, g=g, bk=bk):
                    def f(e):
                        last = None
                        for cc in range(NC):
                            last = e.matmul(
                                bank(bk, 256),
                                Y[:, cc, YPAD + nt * 128:YPAD + nt * 128 + 128],
                                WQKV[:, cc, 2048 + g * 256:2048 + (g + 1) * 256],
                                start=(cc == 0), stop=(cc == NC - 1))
                        return last
                    return f
                w = gate + list(wq_evs)
                if nt >= 2:
                    w.append(f"{tag}.ve{g}.{nt-2}")
                elif g > 0:
                    w.append(f"{tag}.ve{g-1}.{NT-2+nt}")
                    w += prev_group_pv
                P.op("tensor", mk_v(), wait=w, inc=e_v)
                P.op("vector", lambda e, nt=nt, bk=bk: e.tensor_copy(
                    VG[:, nt, :], bank(bk, 256)), wait=[e_v],
                    inc=f"{tag}.ve{g}.{nt}")
                vev.append(f"{tag}.ve{g}.{nt}")
            group_pv = []
            for ph in range(2):
                p = g * 2 + ph
                rope_done = []
                for which, dst, colbase in (("q", QROT, p * 128),
                                            ("k", KROT, 1024 + p * 128)):
                    for ncq in range(NQC):
                        bk = (2 if which == "q" else 4) + (ncq % 2)
                        e_mm = ev(f"{tag}.qk")
                        def mk_qk(ncq=ncq, colbase=colbase, bk=bk):
                            def f(e):
                                last = None
                                for cc in range(NC):
                                    last = e.matmul(
                                        bank(bk),
                                        WQKV[:, cc, colbase:colbase + 128],
                                        Y[:, cc, YPAD + ncq * QC:
                                          YPAD + ncq * QC + QC],
                                        start=(cc == 0), stop=(cc == NC - 1))
                                return last
                            return f
                        w = gate + list(wq_evs)
                        if ncq >= 2:
                            w.append(f"{tag}.rp{p}{which}{ncq-2}")
                        elif which == "q":
                            w.append(last_exp[1])
                        else:
                            w.append(last_no[0])
                            w.append(last_rc[0])
                        P.op("tensor", mk_qk(), wait=w, inc=e_mm)
                        cs = COS[:, ncq * QC:(ncq + 1) * QC]
                        sn = SIN[:, ncq * QC:(ncq + 1) * QC]
                        pb = bank(bk)
                        e_r1 = ev(f"{tag}.r1")
                        P.op("vector", lambda e, pb=pb, cs=cs: e.tensor_tensor(
                            ROPE1, pb, cs, ALU.mult),
                            wait=[e_mm, "c_cos"], inc=e_r1)
                        e_r2 = ev(f"{tag}.r2")
                        def mk_r2(pb=pb, sn=sn):
                            def f(e):
                                last = None
                                for blk in range(4):
                                    ob = (blk // 2) * 64 + (blk % 2) * 32
                                    sb = (blk // 2) * 64 + (1 - blk % 2) * 32
                                    last = e.tensor_tensor(
                                        ROPE2[ob:ob + 32, :],
                                        pb[sb:sb + 32, :],
                                        sn[ob:ob + 32, :], ALU.mult)
                                return last
                            return f
                        P.op("vector", mk_r2(), wait=[e_mm, "c_sin"], inc=e_r2)
                        e_rp = f"{tag}.rp{p}{which}{ncq}"
                        P.op("vector", lambda e, dst=dst, ncq=ncq: e.tensor_tensor(
                            dst[:, ncq * QC:(ncq + 1) * QC], ROPE1, ROPE2,
                            ALU.add), wait=[e_r1, e_r2], inc=e_rp)
                        rope_done.append(e_rp)
                for qc in range(NQC):
                    exp_evs = []
                    pvk_evs = []

                    def emit_pvk(kb2, ph=ph, qc=qc):
                        e_pvk = ev(f"{tag}.pvk")
                        def mk(kb2=kb2, ph=ph):
                            def f(e):
                                last = None
                                for kt in (2 * kb2, 2 * kb2 + 1):
                                    st = (kt == 0)
                                    sp = (kt == NT - 1)
                                    last = e.matmul(
                                        bank(4)[0:64, :],
                                        VG[:, kt, (ph * 2) * 64:
                                           (ph * 2 + 1) * 64],
                                        STRIP[:, 0, kt, :],
                                        start=st, stop=sp,
                                        tile_position=(0, 0))
                                    last = e.matmul(
                                        bank(5)[64:128, :],
                                        ONES, STRIP[:, 1, kt, :],
                                        start=st, stop=sp,
                                        tile_position=(0, 64))
                                    last = e.matmul(
                                        bank(5)[0:64, :],
                                        ONES, STRIP[:, 0, kt, :],
                                        start=st, stop=sp,
                                        tile_position=(0, 0))
                                    last = e.matmul(
                                        bank(4)[64:128, :],
                                        VG[:, kt, (ph * 2 + 1) * 64:
                                           (ph * 2 + 2) * 64],
                                        STRIP[:, 1, kt, :],
                                        start=st, stop=sp,
                                        tile_position=(0, 64))
                                return last
                            return f
                        w = [exp_evs[2 * kb2], exp_evs[2 * kb2 + 1]] + \
                            vev + ["c_ones"]
                        if kb2 == 0:
                            w.append(last_no[0])
                            w.append(last_rc[0])
                        P.op("tensor", mk(), wait=w, inc=e_pvk)
                        pvk_evs.append(e_pvk)

                    for kb2 in range(8):
                        for h in range(2):
                            e_s = ev(f"{tag}.sc")
                            def mk_s(h=h, kb2=kb2, qc=qc):
                                def f(e):
                                    last = None
                                    for j in range(2):
                                        kb = kb2 * 2 + j
                                        last = e.matmul(
                                            bank(2 * h + j),
                                            KROT[h * 64:(h + 1) * 64,
                                                 kb * 128:(kb + 1) * 128],
                                            QROT[h * 64:(h + 1) * 64,
                                                 qc * QC:(qc + 1) * QC],
                                            start=True, stop=True,
                                            tile_position=(h * 64, 0))
                                    return last
                                return f
                            w = list(rope_done) + [last_exp[h]]
                            P.op("tensor", mk_s(), wait=w, inc=e_s)
                            e_x = f"{tag}.ex{p}.{qc}.{h}.{kb2}"
                            P.op("scalar", lambda e, h=h, kb2=kb2:
                                 e.activation(
                                     out=STRIP[:, h, kb2 * 2:kb2 * 2 + 2, :],
                                     in_=psF[:, h * 1024:h * 1024 + 1024],
                                     func=AF.Exp, scale=0.125),
                                 wait=[e_s], inc=e_x)
                            last_exp[h] = e_x
                            exp_evs.append(e_x)
                        if kb2 >= 1:
                            emit_pvk(kb2 - 1)
                    emit_pvk(7)
                    e_rc = f"{tag}.rc{p}.{qc}"
                    P.op("vector", lambda e: e.reciprocal(RD, bank(5)),
                         wait=[pvk_evs[-1]], inc=e_rc)
                    last_rc[0] = e_rc
                    e_no = f"{tag}.no{p}.{qc}"
                    P.op("vector", lambda e, p=p, qc=qc: e.tensor_tensor(
                        ATTO[:, p, qc * QC:(qc + 1) * QC], bank(4), RD,
                        ALU.mult), wait=[e_rc, pvk_evs[-1]], inc=e_no)
                    last_no[0] = e_no
                    no_evs.append(e_no)
                group_pv.append(f"{tag}.no{p}.{NQC-1}")
            prev_group_pv = group_pv
        wproj_ev = load_wproj(tag, wait=[no_evs[-1]])
        xrel = []
        for t in range(NT):
            dma("sync", XN[:, t, :],
                xspill[b].rearrange("(t p) d -> p t d", p=128)[:, t, :],
                wait=[no_evs[-1]] if t == 0 else [], inc=f"{tag}.xr{t}")
            xrel.append(f"{tag}.xr{t}")
        adds = []
        for qc in range(NQC):
            for mt in range(NC):
                bk = mt % 2
                e_mm = ev(f"{tag}.pj")
                def mk_pj(mt=mt, qc=qc, bk=bk):
                    def f(e):
                        last = None
                        for cc in range(NC):
                            last = e.matmul(
                                bank(bk), WPROJ[:, cc, mt * 128:(mt + 1) * 128],
                                ATTO[:, cc, qc * QC:(qc + 1) * QC],
                                start=(cc == 0), stop=(cc == NC - 1))
                        return last
                    return f
                w = list(wproj_ev) + [no_evs[-1]]
                if mt >= 2:
                    w.append(f"{tag}.pc{qc}.{mt-2}")
                elif qc > 0:
                    w.append(f"{tag}.pc{qc-1}.{mt+6}")
                P.op("tensor", mk_pj(), wait=w, inc=e_mm)
                P.op("vector", lambda e, slot=mt % 4, bk=bk: e.tensor_copy(
                    PROD[:, slot, :], bank(bk)), wait=[e_mm],
                    inc=f"{tag}.pc{qc}.{mt}")
                if mt % 4 == 3:
                    half = mt // 4
                    for tb in range(4):
                        bkT = 2 + (tb % 2)
                        e_tp = f"{tag}.ptp{qc}.{half}.{tb}"
                        def mk_tp(tb=tb, bkT=bkT):
                            def f(e):
                                last = None
                                for i in range(4):
                                    last = e.transpose(
                                        psF[:, bkT * 512 + i * 128:
                                            bkT * 512 + i * 128 + 128],
                                        PROD[:, i, tb * 128:(tb + 1) * 128], IDF)
                                return last
                            return f
                        w = [f"{tag}.pc{qc}.{half*4+j}" for j in range(4)]
                        w.append("c_idf")
                        if tb >= 2:
                            w.append(f"{tag}.pa{qc}.{half}.{tb-2}")
                        elif half == 1:
                            w.append(f"{tag}.pa{qc}.0.{tb+2}")
                        elif qc > 0:
                            w.append(f"{tag}.pa{qc-1}.1.{tb+2}")
                        P.op("tensor", mk_tp(), wait=w, inc=e_tp)
                        e_ad = f"{tag}.pa{qc}.{half}.{tb}"
                        t_idx = qc * 4 + tb
                        P.op("vector", lambda e, t_idx=t_idx, half=half,
                             bkT=bkT: e.tensor_tensor(
                                 XN[:, t_idx, half * 512:half * 512 + 512],
                                 bank(bkT),
                                 XN[:, t_idx, half * 512:half * 512 + 512],
                                 ALU.add), wait=[e_tp, xrel[t_idx]], inc=e_ad)
                        adds.append(e_ad)
        return adds

    ffn_w_ev = load_ffn_w("init")
    out_evs = []
    prev_out = None
    wf = ffn_w_ev
    for b in range(B_PER):
        xev = load_x(b, f"b{b}", wait=[prev_out] if prev_out else [])
        y1 = ln_phase(b, 1, [xev])
        f1 = ffn_phase(b, 1, wf, y1, [xev])
        spill_evs = []
        for c in range(NQC):
            dma("sync",
                xspill[b].rearrange("(g p) d -> p g d", p=128)[:, c * 4:c * 4 + 4, :],
                XN[:, c * 4:c * 4 + 4, :],
                wait=[f1[c * 8 + 7]], inc=f"spill{b}.{c}")
            spill_evs.append(f"spill{b}.{c}")
        y2 = ln_phase(b, 2, [f1[-1]])
        wq = load_wqkv(f"b{b}", wait=[f1[-1]])
        a2 = attn_phase(b, y2, wq, spill_evs[-1])
        wf = load_ffn_w(f"s3b{b}", wait=[a2[-1]])
        y3 = ln_phase(b, 3, [a2[-1]])
        f3 = ffn_phase(b, 3, wf, y3, a2[-8:])
        for c in range(NQC):
            dma("sync",
                out_d[b].rearrange("(g p) d -> p g d", p=128)[:, c * 4:c * 4 + 4, :],
                XN[:, c * 4:c * 4 + 4, :],
                wait=[f3[c * 8 + 7]], inc=f"out{b}.{c}")
        out_evs.append(f"out{b}.{NQC-1}")
        prev_out = f"out{b}.{NQC-1}"

    P.op("sync", lambda e: e.nop(), wait=out_evs)

    from contextlib import ExitStack
    with ExitStack() as ctx:
        e = ctx.enter_context
        sems = {name: e(nc.semaphore(f"sem_{name}")) for name in ENGINES}
        blk = e(nc.Block())

        @blk.tensor
        def _(te):
            P.emit("tensor", te, sems)

        @blk.vector
        def _(ve):
            P.emit("vector", ve, sems)

        @blk.scalar
        def _(sc):
            P.emit("scalar", sc, sems)

        @blk.sync
        def _(sy):
            P.emit("sync", sy, sems)

        @blk.gpsimd
        def _(gp):
            P.emit("gpsimd", gp, sems)

    return nc


_NC_CACHE = None


def kernel(L0_attn_x, L0_attn_w_qkv, L0_attn_w_proj, L0_ffn_W_r, L0_ffn_W_k,
           L0_ffn_W_v, L0_ffn_mu_r, L0_ffn_mu_k):
    global _NC_CACHE
    x = np.ascontiguousarray(np.asarray(L0_attn_x, dtype=np.float32))
    wqkv = np.asarray(L0_attn_w_qkv, dtype=np.float32)
    wproj = np.ascontiguousarray(np.asarray(L0_attn_w_proj, dtype=np.float32))
    wrT = np.ascontiguousarray(np.asarray(L0_ffn_W_r, dtype=np.float32).T)
    wkT = np.ascontiguousarray(np.asarray(L0_ffn_W_k, dtype=np.float32).T)
    wvT = np.ascontiguousarray(np.asarray(L0_ffn_W_v, dtype=np.float32).T)
    mur = np.ascontiguousarray(
        np.asarray(L0_ffn_mu_r, np.float32).reshape(NC, 128).T)
    muk = np.ascontiguousarray(
        np.asarray(L0_ffn_mu_k, np.float32).reshape(NC, 128).T)

    src = np.empty(HD, np.int64)
    src[0:32] = np.arange(0, 64, 2)
    src[32:64] = np.arange(1, 64, 2)
    colperm = np.concatenate([h * HD + src for h in range(H)])
    wqkv_p = np.ascontiguousarray(np.concatenate(
        [wqkv[:, 0:D][:, colperm], wqkv[:, D:2 * D][:, colperm],
         wqkv[:, 2 * D:3 * D]], axis=1))

    inv = 1.0 / 10000.0 ** (np.arange(0, HD, 2, dtype=np.float64) / HD)
    ang = np.arange(T, dtype=np.float64)[None, :] * inv[:, None]
    cos32 = np.cos(ang).astype(np.float32)
    sin32 = np.sin(ang).astype(np.float32)
    costab = np.ascontiguousarray(np.tile(np.concatenate([cos32, cos32], 0),
                                          (2, 1)))
    sintab = np.ascontiguousarray(np.tile(np.concatenate([-sin32, sin32], 0),
                                          (2, 1)))
    identb = np.eye(128, dtype=ml_dtypes.bfloat16)
    identf = np.eye(128, dtype=np.float32)
    onesb = np.ones((128, 64), dtype=ml_dtypes.bfloat16)

    if _NC_CACHE is None:
        _NC_CACHE = build_program()
    nc = _NC_CACHE

    in_maps = []
    for c in range(NCORES):
        in_maps.append({
            "x": np.ascontiguousarray(x[c * B_PER:(c + 1) * B_PER]),
            "wqkv": wqkv_p, "wproj": wproj,
            "wrT": wrT, "wkT": wkT, "wvT": wvT,
            "mur": mur, "muk": muk,
            "costab": costab, "sintab": sintab,
            "identb": identb, "identf": identf, "onesb": onesb,
        })
    res = run_bass_kernel_spmd(nc, in_maps, core_ids=list(range(NCORES)))
    outs = [np.asarray(res.results[c]["out"], dtype=np.float32)
            for c in range(NCORES)]
    return np.concatenate(outs, axis=0)
